# version 20
# baseline (speedup 1.0000x reference)
import os
import numpy as np
import concourse.bass as bass
import concourse.mybir as mybir
from concourse.bass_utils import run_bass_kernel_spmd

F32 = mybir.dt.float32
BF16 = mybir.dt.bfloat16
AF = mybir.ActivationFunctionType
ALU = mybir.AluOpType
AX = mybir.AxisListType

NCORES = 8
B = 16
BPC = B // NCORES
S = 2048
E = 1024
H = 16
HD = 64
DC = 256
DR = 128
I = 2816
NT = S // 128
EC = E // 128
IC = I // 128


class _Step:
    __slots__ = ("eng", "emit", "dma_ch", "deps", "needs_inc", "inc_idx",
                 "ch_idx", "tag", "phase", "raw_ids", "pos")

    def __init__(self, eng, emit, dma_ch, tag):
        self.eng = eng
        self.emit = emit
        self.dma_ch = dma_ch
        self.deps = []
        self.needs_inc = False
        self.inc_idx = None
        self.ch_idx = None
        self.tag = tag


class Prog:
    ENGINES = ("tensor", "vector", "scalar", "gpsimd", "sync")

    def __init__(self, nc):
        self.nc = nc
        self.steps = []
        self.last_write = {}
        self.readers = {}
        self.channels = {}
        self.ch_counts = {}
        self.barrier = None
        self.phase = "init"
        self._eng_pos = {}

    def channel(self, name, eng, pool=1):
        for k in range(pool):
            self.channels[f"{name}#{k}"] = eng
            self.ch_counts[f"{name}#{k}"] = 0
        self._pools = getattr(self, "_pools", {})
        self._pools[name] = [pool, 0]

    def add(self, eng, emit, r=(), w=(), dma_ch=None, tag=""):
        s = _Step(eng, emit, dma_ch, tag)
        s.phase = self.phase
        deps = {}
        raw = set()
        for bk in r:
            lw = self.last_write.get(bk)
            if lw is not None:
                deps[id(lw)] = lw
                raw.add(id(lw))
            self.readers.setdefault(bk, []).append(s)
        for bk in w:
            lw = self.last_write.get(bk)
            if lw is not None:
                deps[id(lw)] = lw
            for rd in self.readers.get(bk, ()):
                deps[id(rd)] = rd
            self.last_write[bk] = s
            self.readers[bk] = []
        if self.barrier is not None:
            deps[id(self.barrier)] = self.barrier
        deps.pop(id(s), None)
        s.deps = list(deps.values())
        s.raw_ids = raw
        s.pos = self._eng_pos.get(eng, 0)
        self._eng_pos[eng] = s.pos + 1
        if dma_ch is not None:
            self.ch_counts[dma_ch] += 1
            s.ch_idx = self.ch_counts[dma_ch]
        self.steps.append(s)
        return s

    def t(self, emit, r=(), w=(), tag=""):
        return self.add("tensor", emit, r, w, tag=tag)

    def v(self, emit, r=(), w=(), tag=""):
        return self.add("vector", emit, r, w, tag=tag)

    def s(self, emit, r=(), w=(), tag=""):
        return self.add("scalar", emit, r, w, tag=tag)

    def g(self, emit, r=(), w=(), tag=""):
        return self.add("gpsimd", emit, r, w, tag=tag)

    def y(self, emit, r=(), w=(), tag=""):
        return self.add("sync", emit, r, w, tag=tag)

    def dma(self, ch, emit, r=(), w=(), tag=""):
        pool, nxt = self._pools[ch]
        self._pools[ch][1] = (nxt + 1) % pool
        sub = f"{ch}#{nxt}"
        return self.add(self.channels[sub], emit, r, w, dma_ch=sub, tag=tag)

    def set_barrier(self):
        allk = list(self.last_write.keys())
        old = self.barrier
        self.barrier = None
        b = self.add("sync", lambda e: e.nop(), r=allk, tag="barrier")
        if old is not None:
            b.deps.append(old)
        self.barrier = b

    def finalize_and_emit(self, probe=False, piece_table=None):
        nc = self.nc
        strict = os.environ.get("MK_STRICT", "0") == "1"
        ss = set((os.environ.get("MK_SS", "vector,scalar,gpsimd")).split(",")) - {""}
        self._ss = ss
        self._knear = int(os.environ.get("MK_K", "3"))

        def near_raw(s, d):
            return (d.eng == s.eng and d.eng in ss and id(d) in s.raw_ids
                    and s.pos - d.pos <= self._knear)
        self._near_raw = near_raw
        self.prefix = {}
        if piece_table is not None:
            run = {}
            for (ch, idx), pieces in sorted(piece_table.items()):
                run[ch] = run.get(ch, 0) + pieces
                self.prefix[(ch, idx)] = run[ch]
        for s in self.steps:
            for d in s.deps:
                if d.dma_ch is None and (strict or d.eng != s.eng
                                         or near_raw(s, d)):
                    d.needs_inc = True
        inc_counters = {e: 0 for e in self.ENGINES}
        for s in self.steps:
            if s.dma_ch is None and s.needs_inc:
                inc_counters[s.eng] += 1
                s.inc_idx = inc_counters[s.eng]
        self._slack = int(os.environ.get("MK_SLACK", "0"))
        self._totals = dict(inc_counters)
        self.eng_sems = {}
        self._sem_guards = []
        for e in self.ENGINES:
            if inc_counters[e]:
                g = nc.semaphore("sem_" + e)
                self.eng_sems[e] = g.__enter__()
                self._sem_guards.append(g)
        self.ch_sems = {}
        for chn in self.channels:
            if self.ch_counts[chn]:
                g = nc.semaphore("sem_" + chn)
                self.ch_sems[chn] = g.__enter__()
                self._sem_guards.append(g)
        if probe:
            g = nc.semaphore("sem_sentinel")
            self.sentinel = g.__enter__()
            self._sem_guards.append(g)
        per_eng = {e: [] for e in self.ENGINES}
        for s in self.steps:
            per_eng[s.eng].append(s)

        def make_body(steps):
            def body(eng):
                satisfied = {}

                def wait(sem, val):
                    if satisfied.get(id(sem), -1) >= val:
                        return
                    eng.wait_ge(sem, val)
                    satisfied[id(sem)] = val

                import os as _os
                dbg = _os.environ.get("MK_TRACE") == "1"
                prev_inc = [None]
                def chval(ch, idx):
                    if idx <= 0:
                        return 0
                    return 16 * self.prefix.get((ch, idx), idx)

                for s in steps:
                    for d in s.deps:
                        if d.dma_ch is not None:
                            if dbg:
                                print(f"[{s.eng}] wait {d.dma_ch} >= {chval(d.dma_ch, d.ch_idx)}")
                            wait(self.ch_sems[d.dma_ch], chval(d.dma_ch, d.ch_idx))
                        elif d.eng != s.eng:
                            if dbg:
                                print(f"[{s.eng}] wait {d.eng} >= {d.inc_idx} (dep {d.tag})")
                            wait(self.eng_sems[d.eng],
                                 min(d.inc_idx + self._slack,
                                     self._totals[d.eng]))
                        elif ((strict or self._near_raw(s, d))
                              and d.inc_idx is not None):
                            wait(self.eng_sems[d.eng], d.inc_idx)
                    if s.dma_ch is not None and s.ch_idx > 1:
                        wait(self.ch_sems[s.dma_ch], chval(s.dma_ch, s.ch_idx - 1))
                    with nc.named_scope(s.phase):
                        inst = s.emit(eng)
                    if s.dma_ch is not None:
                        if dbg:
                            print(f"[{s.eng}] DMA {s.tag} inc {s.dma_ch}")
                        inst.then_inc(self.ch_sems[s.dma_ch], 16)
                        if probe:
                            eng.sem_inc(self.sentinel, 1)
                    elif s.needs_inc:
                        if dbg:
                            print(f"[{s.eng}] OP {s.tag} inc -> {s.inc_idx}")
                        inst.then_inc(self.eng_sems[s.eng], 1)
                    elif dbg:
                        print(f"[{s.eng}] OP {s.tag}")
            return body

        with nc.Block() as block:
            for e in self.ENGINES:
                if per_eng[e]:
                    getattr(block, e)(make_body(per_eng[e]))
        table = None
        if probe:
            table = {}
            dma_steps_by_eng = {e: [st for st in per_eng[e] if st.dma_ch]
                                for e in self.ENGINES}
            cursor = {e: 0 for e in self.ENGINES}
            pending = {}
            eng_updates = {}
            for inst in nc.all_instructions():
                si = getattr(inst, "sync_info", None)
                if si is None:
                    continue
                for u in si.on_update:
                    nm = u.ant_name or ""
                    if nm.startswith("sem_") and "#" in nm:
                        pending[nm[4:]] = pending.get(nm[4:], 0) + 1
                    elif nm == "sem_sentinel":
                        done = False
                        for e in self.ENGINES:
                            lst = dma_steps_by_eng[e]
                            k = cursor[e]
                            if k < len(lst) and pending.get(lst[k].dma_ch, 0) > 0:
                                st = lst[k]
                                table[(st.dma_ch, st.ch_idx)] = pending.pop(st.dma_ch)
                                cursor[e] += 1
                                done = True
                                break
                        assert done, f"sentinel attribution failed {pending}"
                    elif nm.startswith("sem_"):
                        eng_updates[nm] = eng_updates.get(nm, 0) + 1
            for e in self.ENGINES:
                exp = sum(1 for st in per_eng[e]
                          if st.dma_ch is None and st.needs_inc)
                got = eng_updates.get("sem_" + e, 0)
                assert got == exp, f"engine sem count mismatch {e}: {got} != {exp}"
        for g in self._sem_guards:
            g.__exit__(None, None, None)
        return table


class MK:
    def __init__(self, nc):
        self.nc = nc
        self.P = Prog(nc)
        self.P.channel("cin", "sync", pool=2)
        self.P.channel("cw", "sync", pool=2)
        self.P.channel("csp", "gpsimd", pool=3)
        self.P.channel("cld", "scalar", pool=3)
        self.P.channel("cout", "gpsimd", pool=2)
        self._guards = []

    def sb(self, name, shape, dtype):
        g = self.nc.sbuf_tensor(name, shape, dtype)
        t = g.__enter__()
        self._guards.append((name, g))
        return t

    def ps(self, name, shape, dtype=F32):
        g = self.nc.psum_tensor(name, shape, dtype)
        t = g.__enter__()
        self._guards.append((name, g))
        return t

    def free(self, names):
        self.P.set_barrier()
        nameset = set(names)
        keep = []
        for nm, g in reversed(self._guards):
            if nm in nameset:
                g.__exit__(None, None, None)
                nameset.discard(nm)
            else:
                keep.append((nm, g))
        assert not nameset, f"not found: {nameset}"
        self._guards = list(reversed(keep))

    def close_all(self):
        for nm, g in reversed(self._guards):
            g.__exit__(None, None, None)
        self._guards = []

    def emit_quant(self, x_ap, out_ap, F, in_keys, out_key, qd=None):
        P = self.P
        q = qd if qd is not None else self.q
        pfx = q.get("pfx", "q")
        G = F // 16
        mx, s_, r_, nsp = q["mx"], q["s"], q["r"], q["nsp"]
        t_, ca, cb, cc, c1, sgn, sq2 = (q["t"], q["ca"], q["cb"], q["cc"],
                                        q["c1"], q["sgn"], q["sq"])
        kx = list(in_keys)
        x3 = x_ap.rearrange("p (g k) -> p g k", k=16)
        P.v(lambda e: e.tensor_reduce(out=mx[:, :G], in_=x3, axis=AX.X,
                                      op=ALU.max, apply_absolute_value=True),
            r=kx + [pfx + ".mx"], w=[pfx + ".mx"], tag=pfx + ".mx")
        P.v(lambda e: e.tensor_scalar(out=s_[:, :G], in0=mx[:, :G],
                                      scalar1=1.0 / 6.0, scalar2=1e-12,
                                      op0=ALU.mult, op1=ALU.max),
            r=[pfx + ".mx", pfx + ".s"], w=[pfx + ".s"], tag=pfx + ".s")
        P.v(lambda e: e.reciprocal(out=r_[:, :G], in_=s_[:, :G]),
            r=[pfx + ".s", pfx + ".r"], w=[pfx + ".r"], tag=pfx + ".r")
        P.v(lambda e: e.tensor_scalar_mul(out=nsp[:, :G], in0=s_[:, :G],
                                          scalar1=0.5),
            r=[pfx + ".s", pfx + ".nsp"], w=[pfx + ".nsp"], tag=pfx + ".nsp")
        P.s(lambda e: e.activation(out=sgn[:, :F], in_=x_ap, func=AF.Sign),
            r=kx + [pfx + ".sgn"], w=[pfx + ".sgn"], tag=pfx + ".sgn")
        P.s(lambda e: e.activation(out=t_[:, :F], in_=x_ap, func=AF.Square),
            r=kx + [pfx + ".t"], w=[pfx + ".t"], tag=pfx + ".x2")
        P.s(lambda e: e.activation(out=s_[:, :G], in_=r_[:, :G],
                                   func=AF.Square),
            r=[pfx + ".r", pfx + ".s"], w=[pfx + ".s"], tag=pfx + ".r2")
        r2b = s_[:, :G].unsqueeze(2).broadcast_to([128, G, 16])
        P.v(lambda e: e.tensor_tensor(
                out=sq2[:, :F].rearrange("p (g k) -> p g k", k=16),
                in0=t_[:, :F].rearrange("p (g k) -> p g k", k=16),
                in1=r2b, op=ALU.mult),
            r=[pfx + ".t", pfx + ".s", pfx + ".sq"], w=[pfx + ".sq"], tag=pfx + ".sq")
        P.v(lambda e: e.tensor_scalar(out=ca[:, :F], in0=sq2[:, :F],
                                      scalar1=0.0625, scalar2=None,
                                      op0=ALU.is_gt),
            r=[pfx + ".sq", pfx + ".ca"], w=[pfx + ".ca"], tag=pfx + ".c")
        P.v(lambda e: e.tensor_scalar(out=c1[:, :F], in0=sq2[:, :F],
                                      scalar1=0.5625, scalar2=None,
                                      op0=ALU.is_gt),
            r=[pfx + ".sq", pfx + ".c1"], w=[pfx + ".c1"], tag=pfx + ".c")
        P.v(lambda e: e.tensor_tensor(out=ca[:, :F], in0=ca[:, :F],
                                      in1=c1[:, :F], op=ALU.add),
            r=[pfx + ".ca", pfx + ".c1"], w=[pfx + ".ca"], tag=pfx + ".add")
        P.v(lambda e: e.tensor_scalar(out=cc[:, :F], in0=sq2[:, :F],
                                      scalar1=1.5625, scalar2=None,
                                      op0=ALU.is_gt),
            r=[pfx + ".sq", pfx + ".cc"], w=[pfx + ".cc"], tag=pfx + ".c")
        P.v(lambda e: e.tensor_scalar(out=c1[:, :F], in0=sq2[:, :F],
                                      scalar1=3.0625, scalar2=None,
                                      op0=ALU.is_gt),
            r=[pfx + ".sq", pfx + ".c1"], w=[pfx + ".c1"], tag=pfx + ".c")
        P.v(lambda e: e.tensor_tensor(out=cc[:, :F], in0=cc[:, :F],
                                      in1=c1[:, :F], op=ALU.add),
            r=[pfx + ".cc", pfx + ".c1"], w=[pfx + ".cc"], tag=pfx + ".add")
        P.v(lambda e: e.tensor_tensor(out=ca[:, :F], in0=ca[:, :F],
                                      in1=cc[:, :F], op=ALU.add),
            r=[pfx + ".ca", pfx + ".cc"], w=[pfx + ".ca"], tag=pfx + ".add")
        P.v(lambda e: e.tensor_scalar(out=cb[:, :F], in0=sq2[:, :F],
                                      scalar1=6.25, scalar2=None,
                                      op0=ALU.is_gt),
            r=[pfx + ".sq", pfx + ".cb"], w=[pfx + ".cb"], tag=pfx + ".c")
        P.v(lambda e: e.tensor_scalar(out=c1[:, :F], in0=sq2[:, :F],
                                      scalar1=12.25, scalar2=None,
                                      op0=ALU.is_gt),
            r=[pfx + ".sq", pfx + ".c1"], w=[pfx + ".c1"], tag=pfx + ".c")
        P.v(lambda e: e.tensor_tensor(out=cb[:, :F], in0=cb[:, :F],
                                      in1=c1[:, :F], op=ALU.add),
            r=[pfx + ".cb", pfx + ".c1"], w=[pfx + ".cb"], tag=pfx + ".add")
        P.v(lambda e: e.tensor_scalar(out=c1[:, :F], in0=sq2[:, :F],
                                      scalar1=25.0, scalar2=2.0,
                                      op0=ALU.is_gt, op1=ALU.mult),
            r=[pfx + ".sq", pfx + ".c1"], w=[pfx + ".c1"], tag=pfx + ".c")
        P.v(lambda e: e.tensor_tensor(out=cb[:, :F], in0=cb[:, :F],
                                      in1=c1[:, :F], op=ALU.add),
            r=[pfx + ".cb", pfx + ".c1"], w=[pfx + ".cb"], tag=pfx + ".add")
        P.v(lambda e: e.tensor_scalar(out=cb[:, :F], in0=cb[:, :F],
                                      scalar1=2.0, scalar2=None,
                                      op0=ALU.mult),
            r=[pfx + ".cb"], w=[pfx + ".cb"], tag=pfx + ".sc")
        P.v(lambda e: e.tensor_tensor(out=ca[:, :F], in0=ca[:, :F],
                                      in1=cb[:, :F], op=ALU.add),
            r=[pfx + ".ca", pfx + ".cb"], w=[pfx + ".ca"], tag=pfx + ".add")
        spb = nsp[:, :G].unsqueeze(2).broadcast_to([128, G, 16])
        P.v(lambda e: e.tensor_tensor(
                out=c1[:, :F].rearrange("p (g k) -> p g k", k=16),
                in0=ca[:, :F].rearrange("p (g k) -> p g k", k=16),
                in1=spb, op=ALU.mult),
            r=[pfx + ".ca", pfx + ".nsp", pfx + ".c1"], w=[pfx + ".c1"], tag=pfx + ".y0")
        P.g(lambda e: e.tensor_tensor(out=out_ap, in0=c1[:, :F],
                                      in1=sgn[:, :F], op=ALU.mult),
            r=[pfx + ".c1", pfx + ".sgn", out_key], w=[out_key], tag=pfx + ".y")


def build_program(probe=False, piece_table=None):
    nc = bass.Bass("TRN2", target_bir_lowering=False)
    mk = MK(nc)
    P = mk.P

    d_x = nc.dram_tensor("x", [BPC * S, E], F32, kind="ExternalInput").ap()
    d_y = nc.dram_tensor("y", [BPC * S, E], F32, kind="ExternalOutput").ap()
    dw = {}
    for nm, shp in [("wq", [E, DC]), ("bq", [DC]), ("wqc", [DC, E - DR]),
                    ("bqc", [E - DR]), ("wqr", [E, DR]), ("bqr", [DR]),
                    ("wkv", [E, DC]), ("bkv", [DC]), ("wkr", [E, DR]),
                    ("bkr", [DR]), ("wkc", [DC, E - DR]), ("bkc", [E - DR]),
                    ("wv", [DC, E]), ("bv", [E]), ("wo", [E, E]), ("bo", [E]),
                    ("wg", [I, E]), ("wu", [I, E]), ("wd", [E, I])]:
        dw[nm] = nc.dram_tensor(nm, shp, F32, kind="ExternalInput").ap()
    d_cos = nc.dram_tensor("costab", [128, S], F32, kind="ExternalInput").ap()
    d_sin = nc.dram_tensor("sintab", [128, S], F32, kind="ExternalInput").ap()
    d_ide = nc.dram_tensor("identity", [128, 128], F32, kind="ExternalInput").ap()
    d_ltm = nc.dram_tensor("ltmask", [128, 128], F32, kind="ExternalInput").ap()

    _dbg = os.environ.get("MK_DEBUG", "") == "1"
    _kind = "ExternalOutput" if _dbg else "Internal"
    s_xqT = nc.dram_tensor("s_xqT", [BPC, 128, NT * E], BF16, kind=_kind).ap()
    s_qp = nc.dram_tensor("s_qp", [BPC, 8, 128, S], BF16, kind=_kind).ap()
    s_kp = nc.dram_tensor("s_kp", [BPC, 8, 128, S], BF16, kind=_kind).ap()
    s_v = nc.dram_tensor("s_v", [BPC, NT, 128, E], BF16, kind=_kind).ap()
    s_at = nc.dram_tensor("s_at", [BPC, H, HD, S], BF16, kind=_kind).ap()
    s_wg = nc.dram_tensor("s_wg", [E, I], BF16, kind=_kind).ap()
    s_wu = nc.dram_tensor("s_wu", [E, I], BF16, kind=_kind).ap()
    s_wd = nc.dram_tensor("s_wd", [I, E], BF16, kind=_kind).ap()
    s_ng = nc.dram_tensor("s_ng", [BPC, H], F32, kind=_kind).ap()
    d_dbg = (nc.dram_tensor("dbg", [8, 128, 1024], F32, kind="ExternalOutput").ap()
             if _dbg else None)

    def dbg_snap(k, ap_, rkeys, col0=0):
        if d_dbg is not None:
            P.dma("cout", lambda e, k=k, ap_=ap_, col0=col0: e.dma_start(
                    out=d_dbg[k][0:ap_.shape[0], col0:col0 + ap_.shape[-1]],
                    in_=ap_),
                  r=list(rkeys), w=[f"dbg{k}"], tag=f"dbg.{k}")
    mk.dbg_snap = dbg_snap

    ident_f = mk.sb("ident_f", [128, 128], F32)
    ident_b = mk.sb("ident_b", [128, 128], BF16)
    ltm_b = mk.sb("ltm_b", [128, 128], BF16)
    ones_bf = mk.sb("ones_bf", [1, 128], BF16)
    bo_row = mk.sb("bo_row", [1, E], BF16)
    cst = mk.sb("cst", [128, 1], F32)
    nc.const_aps.register(F32, 1e-5, cst[:, 0:1])
    wo_s = mk.sb("wo_s", [128, EC * E], BF16)

    def alloc_qset(pfx, Fw):
        d = {"pfx": pfx}
        names = []
        gw = Fw // 16
        for nm, w, dt in (("mx", gw, F32), ("s", gw, F32), ("r", gw, F32),
                          ("nsp", gw, F32), ("t", Fw, F32), ("sq", Fw, F32),
                          ("sgn", Fw, BF16), ("ca", Fw, BF16),
                          ("cb", Fw, BF16), ("cc", Fw, BF16),
                          ("c1", Fw, BF16)):
            d[nm] = mk.sb(f"{pfx}_{nm}", [128, w], dt)
            names.append(f"{pfx}_{nm}")
        return d, names

    pA = mk.ps("pA", [128, 1024], F32)
    pB = mk.ps("pB", [128, 1024], F32)
    pAV = mk.ps("pAV", [65, S], F32)
    KA0, KA1, KB0, KB1 = "pA0", "pA1", "pB0", "pB1"

    cosT = mk.sb("cosT", [128, S], F32)
    sinT = mk.sb("sinT", [128, S], F32)
    stgw = mk.sb("stgw", [128, E], F32)
    w_sb = {"wo": wo_s}
    wspec = {"wq": (EC, DC), "wqc": (2, E - DR), "wqr": (EC, DR),
             "wkv": (EC, DC), "wkr": (EC, DR), "wkc": (2, E - DR),
             "wv": (2, E)}
    for nm, (kchunks, odim) in wspec.items():
        w_sb[nm] = mk.sb(nm + "_s", [128, kchunks * odim], BF16)
    b_sb = {}
    for nm, n in [("bq", DC), ("bqc", E - DR), ("bqr", DR), ("bkv", DC),
                  ("bkr", DR), ("bkc", E - DR), ("bv", E)]:
        b_sb[nm] = mk.sb(nm + "_s", [128, n // 128], F32)

    P.dma("cw", lambda e: e.dma_start(out=ident_f[:], in_=d_ide), w=["ident_f"])
    P.v(lambda e: e.tensor_copy(out=ident_b[:], in_=ident_f[:]),
        r=["ident_f", "ident_b"], w=["ident_b"])
    P.dma("cw", lambda e: e.dma_start(out=cosT[:], in_=d_cos), w=["cosT"])
    P.dma("cw", lambda e: e.dma_start(out=sinT[:], in_=d_sin), w=["sinT"])
    P.v(lambda e: e.memset(cst[:], 1e-5), w=["cst"])
    P.v(lambda e: e.memset(ones_bf[:], 1.0), w=["ones_bf"])
    wspec2 = dict(wspec)
    wspec2["wo"] = (EC, E)
    if os.environ.get("MK_NOWL", "0") == "1":
        wspec2 = {}
    for nm, (kchunks, odim) in wspec2.items():
        for k in range(kchunks):
            P.dma("cw", lambda e, nm=nm, k=k, odim=odim: e.dma_start(
                    out=stgw[:, 0:odim], in_=dw[nm][128 * k:128 * (k + 1), :]),
                  r=["stgw"], w=["stgw"], tag=f"ldw.{nm}")
            P.v(lambda e, nm=nm, k=k, odim=odim: e.tensor_copy(
                    out=w_sb[nm][:, k * odim:(k + 1) * odim],
                    in_=stgw[:, 0:odim]),
                r=["stgw"], w=[nm + "_s"], tag=f"cvw.{nm}")
    for nm, n in ([] if os.environ.get("MK_NOWL", "0") == "1" else
                  [("bq", DC), ("bqc", E - DR), ("bqr", DR), ("bkv", DC),
                   ("bkr", DR), ("bkc", E - DR), ("bv", E)]):
        P.dma("cw", lambda e, nm=nm, n=n: e.dma_start(
                out=b_sb[nm][:, 0:n // 128],
                in_=dw[nm].rearrange("(k p) -> p k", p=128),
                allow_slow_non_contiguous=True),
              w=[nm + "_s"], tag=f"ldb.{nm}")
    P.dma("cw", lambda e: e.dma_start(out=stgw[0:1, 0:E],
                                       in_=dw["bo"].rearrange("e -> () e")),
          r=["stgw"], w=["stgw"], tag="ldb.bo")
    P.v(lambda e: e.tensor_copy(out=bo_row[:], in_=stgw[0:1, 0:E]),
        r=["stgw"], w=["bo_row"])
    P.dma("cw", lambda e: e.dma_start(out=stgw[:, 0:128], in_=d_ltm),
          r=["stgw"], w=["stgw"])
    P.v(lambda e: e.tensor_copy(out=ltm_b[:], in_=stgw[:, 0:128]),
        r=["stgw"], w=["ltm_b"])

    P.phase = "wquant"

    qw_sets = [alloc_qset("qw0", 1024), alloc_qset("qw1", 1024)]
    wstg2 = [mk.sb("wstg0", [128, 1024], F32), mk.sb("wstg1", [128, 1024], F32)]
    wqf2 = [mk.sb("wqf0", [128, 1024], F32), mk.sb("wqf1", [128, 1024], F32)]
    wqT2 = [mk.sb("wqT0", [128, 1024], BF16), mk.sb("wqT1", [128, 1024], BF16)]
    _pc = [0]

    def quant_weight(dsrc, dscr, rows, cols, nm):
        RC = rows // 128
        pieces = []
        c0 = 0
        while c0 < cols:
            cw = min(1024, cols - c0)
            pieces.append((c0, cw))
            c0 += cw
        for kc in range(RC):
            for c0, cw in pieces:
                par = _pc[0] % 2
                _pc[0] += 1
                wstg, wqf, wqT = wstg2[par], wqf2[par], wqT2[par]
                kst, kqf, kqT = f"wstg{par}", f"wqf{par}", f"wqT{par}"
                slots = ((pA[:, 0:128], KA0), (pA[:, 512:640], KA1)) \
                    if par == 0 else ((pB[:, 0:128], KB0), (pB[:, 512:640], KB1))
                P.dma("cw", lambda e, kc=kc, c0=c0, cw=cw, dsrc=dsrc,
                      wstg=wstg: e.dma_start(
                          out=wstg[:, 0:cw],
                          in_=dsrc[128 * kc:128 * (kc + 1), c0:c0 + cw]),
                      r=[kst], w=[kst], tag=f"qw.ld.{nm}")
                mk.emit_quant(wstg[:, 0:cw], wqf[:, 0:cw], cw, [kst], kqf,
                              qd=qw_sets[par][0])
                for cc in range(cw // 128):
                    pslot, kps = slots[cc % 2]
                    P.t(lambda e, cc=cc, pslot=pslot, wqf=wqf: e.transpose(
                            pslot, wqf[:, cc * 128:(cc + 1) * 128], ident_f[:]),
                        r=[kqf, "ident_f", kps], w=[kps], tag=f"qw.tr.{nm}")
                    P.v(lambda e, cc=cc, pslot=pslot, wqT=wqT: e.tensor_copy(
                            out=wqT[:, cc * 128:(cc + 1) * 128], in_=pslot),
                        r=[kps, kqT], w=[kqT, kps], tag=f"qw.ev.{nm}")
                P.dma("csp", lambda e, kc=kc, c0=c0, cw=cw, dscr=dscr,
                      wqT=wqT: e.dma_start(
                          out=dscr[c0:c0 + cw, 128 * kc:128 * (kc + 1)]
                          .rearrange("(c p) r -> p c r", p=128),
                          in_=wqT[:, 0:cw].rearrange("p (c r) -> p c r", r=128)),
                      r=[kqT], w=[f"scr_{nm}"], tag=f"qw.st.{nm}")

    if os.environ.get("MK_NOW", "0") != "1":
        quant_weight(dw["wg"], s_wg, I, E, "wg")
        quant_weight(dw["wu"], s_wu, I, E, "wu")
        quant_weight(dw["wd"], s_wd, E, I, "wd")
    mk.free(["wstg0", "wstg1", "wqf0", "wqf1", "wqT0", "wqT1"]
            + qw_sets[0][1] + qw_sets[1][1])

    NT1 = int(os.environ.get("MK_P1TILES", str(NT)))

    def phase_p1(b, T):
        xnT, xt, xnf, xqf, lnst, xqTall = (T["xnT"], T["xt"], T["xnf"],
                                           T["xqf"], T["ln_sc"], T["xqTall"])
        for i in range(NT1):
            xb = xt[i % 2]
            kxt = f"xt{i % 2}"
            P.dma("cin", lambda e, i=i, xb=xb, b=b: e.dma_start(
                    out=xb[:], in_=d_x[b * S + 128 * i:b * S + 128 * (i + 1), :]),
                  r=[kxt], w=[kxt], tag=f"p1.ldx.{i}")
            P.v(lambda e, xb=xb, i=i: e.reduce_sum(out=lnst[:, i:i + 1],
                                                   in_=xb[:], axis=AX.X),
                r=[kxt, "lnst"], w=["lnst"], tag="p1.sum")
            P.s(lambda e, xb=xb, i=i: e.activation(
                    out=pB[:, 0:E], in_=xb[:], func=AF.Square,
                    accum_out=lnst[:, 16 + i:17 + i]),
                r=[kxt, KB0, KB1, "lnst"], w=[KB0, KB1, "lnst"], tag="p1.ssq")
        P.v(lambda e: e.tensor_scalar_mul(out=lnst[:, 32:48],
                                          in0=lnst[:, 0:16],
                                          scalar1=-1.0 / E),
            r=["lnst"], w=["lnst"], tag="p1.negm")
        P.s(lambda e: e.activation(out=lnst[:, 48:64], in_=lnst[:, 32:48],
                                   func=AF.Square),
            r=["lnst"], w=["lnst"], tag="p1.m2")
        P.v(lambda e: e.tensor_scalar_mul(out=lnst[:, 0:16],
                                          in0=lnst[:, 16:32],
                                          scalar1=1.0 / E),
            r=["lnst"], w=["lnst"], tag="p1.ex2")
        if b == 0:
            mk.dbg_snap(5, lnst[:, 0:64], ["lnst"])
        P.v(lambda e: e.tensor_tensor(out=lnst[:, 0:16], in0=lnst[:, 0:16],
                                      in1=lnst[:, 48:64], op=ALU.subtract),
            r=["lnst"], w=["lnst"], tag="p1.var")
        if b == 0:
            mk.dbg_snap(6, lnst[:, 0:64], ["lnst"])
        P.s(lambda e: e.activation(out=lnst[:, 48:64], in_=lnst[:, 0:16],
                                   func=AF.Ln, scale=1.0, bias=1e-5),
            r=["lnst", "cst"], w=["lnst"], tag="p1.lnv")
        P.s(lambda e: e.activation(out=lnst[:, 48:64], in_=lnst[:, 48:64],
                                   func=AF.Exp, scale=-0.5),
            r=["lnst"], w=["lnst"], tag="p1.rstd")
        for i in range(NT1):
            xb = xt[i % 2]
            kxt = f"xt{i % 2}"
            P.dma("cin", lambda e, i=i, xb=xb, b=b: e.dma_start(
                    out=xb[:], in_=d_x[b * S + 128 * i:b * S + 128 * (i + 1), :]),
                  r=[kxt], w=[kxt], tag=f"p1.ldx2.{i}")
            P.v(lambda e, xb=xb, i=i: e.tensor_tensor(
                    out=xnf[:], in0=xb[:],
                    in1=lnst[:, 32 + i:33 + i].broadcast_to([128, E]),
                    op=ALU.add),
                r=[kxt, "lnst", "xnf"], w=["xnf"], tag="p1.xnf")
            P.v(lambda e, i=i: e.tensor_tensor(
                    out=xnf[:], in0=xnf[:],
                    in1=lnst[:, 48 + i:49 + i].broadcast_to([128, E]),
                    op=ALU.mult),
                r=["lnst", "xnf"], w=["xnf"], tag="p1.xnf2")
            for c in range(EC):
                kps = KA0 if c % 2 == 0 else KA1
                pslot = pA[:, 0:128] if c % 2 == 0 else pA[:, 512:640]
                P.t(lambda e, c=c, pslot=pslot: e.transpose(
                        pslot, xnf[:, c * 128:(c + 1) * 128], ident_f[:]),
                    r=["xnf", "ident_f", kps], w=[kps], tag=f"p1.trx.{c}")
                P.v(lambda e, c=c, i=i, pslot=pslot: e.tensor_copy(
                        out=xnT[c][:, 128 * i:128 * (i + 1)], in_=pslot),
                    r=[kps, f"xnT{c}"], w=[f"xnT{c}", kps], tag=f"p1.evx.{c}")
            if b == 0 and i == 0:
                mk.dbg_snap(0, xnf[:], ["xnf"])
            mk.emit_quant(xnf[:], xqf[:], E, ["xnf"], "xqf")
            if b == 0 and i == 0:
                mk.dbg_snap(1, mk.q["t"][:, 0:E], ["q.t"])
                mk.dbg_snap(2, mk.q["sq"][:, 0:E], ["q.sq"])
                mk.dbg_snap(3, xqf[:], ["xqf"])
                mk.dbg_snap(4, lnst[:, 0:64], ["lnst"])
            for c in range(EC):
                kps = KA0 if c % 2 == 0 else KA1
                pslot = pA[:, 0:128] if c % 2 == 0 else pA[:, 512:640]
                P.t(lambda e, c=c, pslot=pslot: e.transpose(
                        pslot, xqf[:, c * 128:(c + 1) * 128], ident_f[:]),
                    r=["xqf", "ident_f", kps], w=[kps], tag=f"p1.trq.{c}")
                P.v(lambda e, c=c, i=i, pslot=pslot: e.tensor_copy(
                        out=xqTall[:, i * E + c * 128:i * E + (c + 1) * 128],
                        in_=pslot),
                    r=[kps, "xqTall"], w=["xqTall", kps], tag=f"p1.evq.{c}")
        P.dma("csp", lambda e, b=b, xqTall=xqTall: e.dma_start(
                out=s_xqT[b], in_=xqTall[:]),
              r=["xqTall"], w=[f"s_xqT{b}"], tag="p1.spill")

    def phase_p2(b, T):
        xnT = T["xnT"]
        cqT, ckvT, qrp, krp, rpre = (T["cqT"], T["ckvT"], T["qrp"], T["krp"],
                                     T["rpre"])
        qpair, kpair = T["qpair"], T["kpair"]
        nstat, nrow, ngrow, vevac, vstg = (T["nstat"], T["nrow"], T["ngrow"],
                                           T["vevac"], T["vstg"])
        NC = S // 512
        specA = [("cq", "wq", "bq", 2, DC, KA0, pA[:, 0:512], cqT, "cqT",
                  AF.Gelu),
                 ("ckv", "wkv", "bkv", 2, DC, KA1, pA[:, 512:1024], ckvT,
                  "ckvT", AF.Gelu),
                 ("qr", "wqr", "bqr", 1, DR, KB0, pB[:, 0:512], qrp, "qrp",
                  AF.Identity),
                 ("kr", "wkr", "bkr", 1, DR, KB1, pB[:, 512:1024], krp, "krp",
                  AF.Identity)]
        for n in range(NC):
            for nm, wname, bname, mch, odim, kps, pslot, dstT, dkey, act in specA:
                for m in range(mch):
                    for kc in range(EC):
                        P.t(lambda e, wname=wname, m=m, kc=kc, n=n, odim=odim,
                            pslot=pslot: e.matmul(
                                pslot,
                                w_sb[wname][:, kc * odim + 128 * m:
                                            kc * odim + 128 * m + 128],
                                xnT[kc][:, 512 * n:512 * (n + 1)],
                                start=(kc == 0), stop=(kc == EC - 1)),
                            r=[wname + "_s", f"xnT{kc}", kps], w=[kps],
                            tag=f"p2a.{nm}")
                    if mch == 2:
                        P.s(lambda e, m=m, n=n, dstT=dstT, bname=bname,
                            pslot=pslot, act=act: e.activation(
                                out=dstT[:, m * S + 512 * n:
                                         m * S + 512 * (n + 1)],
                                in_=pslot, func=act,
                                bias=b_sb[bname][:, m:m + 1]),
                            r=[kps, bname + "_s", dkey], w=[dkey, kps],
                            tag=f"p2a.ev.{nm}")
                    else:
                        P.s(lambda e, n=n, dstT=dstT, bname=bname, pslot=pslot:
                            e.activation(
                                out=dstT[:, 512 * n:512 * (n + 1)],
                                in_=pslot, func=AF.Identity,
                                bias=b_sb[bname][:, 0:1]),
                            r=[kps, bname + "_s", dkey], w=[dkey, kps],
                            tag=f"p2a.ev.{nm}")
        for nm, tile, key in [("q", qrp, "qrp"), ("k", krp, "krp")]:
            P.dma("csp", lambda e, tile=tile: e.dma_start(
                    out=rpre[0:64, :], in_=tile[64:128, :]),
                  r=[key, "rpre"], w=["rpre"], tag="p2r.sw0")
            P.dma("csp", lambda e, tile=tile: e.dma_start(
                    out=rpre[64:128, :], in_=tile[0:64, :]),
                  r=[key, "rpre"], w=["rpre"], tag="p2r.sw1")
            P.v(lambda e, tile=tile: e.tensor_tensor(
                    out=tile[:, :], in0=tile[:, :], in1=cosT[:],
                    op=ALU.mult),
                r=[key, "cosT"], w=[key], tag="p2r.c")
            P.v(lambda e: e.tensor_tensor(
                    out=rpre[:, :], in0=rpre[:, :], in1=sinT[:],
                    op=ALU.mult),
                r=["rpre", "sinT"], w=["rpre"], tag="p2r.s")
            P.v(lambda e, tile=tile: e.tensor_tensor(
                    out=tile[0:64, :], in0=tile[0:64, :], in1=rpre[0:64, :],
                    op=ALU.subtract),
                r=[key, "rpre"], w=[key], tag="p2r.sub")
            P.v(lambda e, tile=tile: e.tensor_tensor(
                    out=tile[64:128, :], in0=tile[64:128, :],
                    in1=rpre[64:128, :], op=ALU.add),
                r=[key, "rpre"], w=[key], tag="p2r.add")
        for p in range(8):
            qb, kb = qpair[p % 2], kpair[p % 2]
            kqb, kkb = f"qpair{p % 2}", f"kpair{p % 2}"
            if p < 7:
                for srcT, skey, wname, bname, dst, dkey in [
                        (cqT, "cqT", "wqc", "bqc", qb, kqb),
                        (ckvT, "ckvT", "wkc", "bkc", kb, kkb)]:
                    for n in range(NC):
                        for kc in range(2):
                            P.t(lambda e, p=p, kc=kc, n=n, srcT=srcT,
                                wname=wname: e.matmul(
                                    pB[:, 0:512],
                                    w_sb[wname][:, kc * (E - DR) + 128 * p:
                                                kc * (E - DR) + 128 * p + 128],
                                    srcT[:, kc * S + 512 * n:
                                         kc * S + 512 * (n + 1)],
                                    start=(kc == 0), stop=(kc == 1)),
                                r=[wname + "_s", skey, KB0], w=[KB0],
                                tag=f"p2h.{p}")
                        P.s(lambda e, p=p, n=n, dst=dst, bname=bname:
                            e.activation(
                                out=dst[:, 512 * n:512 * (n + 1)],
                                in_=pB[:, 0:512], func=AF.Gelu,
                                bias=b_sb[bname][:, p:p + 1]),
                            r=[KB0, bname + "_s", dkey], w=[dkey, KB0],
                            tag=f"p2h.ev.{p}")
            else:
                for tile, key, dst, dkey in [(qrp, "qrp", qb, kqb),
                                             (krp, "krp", kb, kkb)]:
                    P.s(lambda e, tile=tile, dst=dst: e.activation(
                            out=dst[:], in_=tile[:], func=AF.Gelu),
                        r=[key, dkey], w=[dkey], tag="p2h.rope")
            for side, buf, bkey in [(0, qb, kqb), (1, kb, kkb)]:
                P.v(lambda e, buf=buf: e.reduce_max(
                        out=nstat[:, 0:1], in_=buf[:], axis=AX.X,
                        apply_absolute_value=True),
                    r=[bkey, "nstat"], w=["nstat"], tag="p2n.mx")
                P.s(lambda e: e.activation(out=nstat[:, 1:2],
                                           in_=nstat[:, 0:1], func=AF.Square),
                    r=["nstat"], w=["nstat"], tag="p2n.sq")
                P.t(lambda e: e.transpose(pB[0:1, 0:128], nstat[:, 1:2],
                                          ident_f[:]),
                    r=["nstat", "ident_f", KB0], w=[KB0], tag="p2n.tr")
                P.v(lambda e, side=side: e.reduce_sum(
                        out=nrow[0:1, side * 2:side * 2 + 1],
                        in_=pB[0:1, 0:64], axis=AX.X),
                    r=[KB0, "nrow"], w=["nrow"], tag="p2n.s0")
                P.v(lambda e, side=side: e.reduce_sum(
                        out=nrow[0:1, side * 2 + 1:side * 2 + 2],
                        in_=pB[0:1, 64:128], axis=AX.X),
                    r=[KB0, "nrow"], w=["nrow", KB0], tag="p2n.s1")
            P.v(lambda e: e.tensor_tensor(out=nrow[0:1, 4:6],
                                          in0=nrow[0:1, 0:2],
                                          in1=nrow[0:1, 2:4], op=ALU.add),
                r=["nrow"], w=["nrow"], tag="p2n.qk")
            P.v(lambda e, p=p: e.tensor_scalar_mul(
                    out=ngrow[0:1, 2 * p:2 * p + 2], in0=nrow[0:1, 4:6],
                    scalar1=-0.0625),
                r=["nrow", "ngrow"], w=["ngrow"], tag="p2n.c")
            if b == 0 and p == 0:
                mk.dbg_snap(7, nstat[:, 0:2], ["nstat"])
                mk.dbg_snap(7, nrow[0:1, 0:8], ["nrow"], col0=8)
            P.dma("csp", lambda e, p=p, qb=qb, b=b: e.dma_start(
                    out=s_qp[b, p], in_=qb[:]),
                  r=[kqb], w=[f"s_qp{b}.{p}"], tag="p2.spq")
            P.dma("csp", lambda e, p=p, kb=kb, b=b: e.dma_start(
                    out=s_kp[b, p], in_=kb[:]),
                  r=[kkb], w=[f"s_kp{b}.{p}"], tag="p2.spk")
        P.dma("csp", lambda e, b=b, ngrow=ngrow: e.dma_start(
                out=s_ng[b].rearrange("h -> () h"), in_=ngrow[0:1, :]),
              r=["ngrow"], w=[f"s_ng{b}"], tag="p2.spng")
        for n in range(NC):
            for m in range(EC):
                for kc in range(2):
                    P.t(lambda e, m=m, kc=kc, n=n: e.matmul(
                            pA[:, 0:512],
                            w_sb["wv"][:, kc * E + 128 * m:
                                       kc * E + 128 * m + 128],
                            ckvT[:, kc * S + 512 * n:kc * S + 512 * (n + 1)],
                            start=(kc == 0), stop=(kc == 1)),
                        r=["wv_s", "ckvT", KA0], w=[KA0], tag="p2v.mm")
                P.s(lambda e, m=m: e.activation(
                        out=vevac[:, 0:512], in_=pA[:, 0:512],
                        func=AF.Identity, bias=b_sb["bv"][:, m:m + 1]),
                    r=[KA0, "bv_s", "vevac"], w=["vevac", KA0], tag="p2v.ev")
                for j in range(4):
                    kps = KA1 if j % 2 == 0 else KB1
                    pslot = pA[:, 512:640] if j % 2 == 0 else pB[:, 512:640]
                    P.t(lambda e, j=j, pslot=pslot: e.transpose(
                            pslot, vevac[:, j * 128:(j + 1) * 128],
                            ident_f[:]),
                        r=["vevac", "ident_f", kps], w=[kps], tag="p2v.tr")
                    P.v(lambda e, m=m, j=j, pslot=pslot: e.tensor_copy(
                            out=vstg[j][:, m * 128:(m + 1) * 128], in_=pslot),
                        r=[kps, f"vstg{j}"], w=[f"vstg{j}", kps],
                        tag="p2v.evt")
            for j in range(4):
                kt = 4 * n + j
                P.dma("csp", lambda e, kt=kt, j=j, b=b, vstg=vstg:
                      e.dma_start(out=s_v[b, kt], in_=vstg[j][:]),
                      r=[f"vstg{j}"], w=[f"s_v{b}.{kt}"], tag="p2v.sp")

    def phase_p3(b, T):
        vres, vaug, ptil, avs, otm, recl, athd, negc = (
            T["vres"], T["vaug"], T["ptil"], T["avs"], T["otm"], T["recl"],
            T["athd"], T["negc"])
        qpair, kpair = T["qpair3"], T["kpair3"]
        P.dma("cld", lambda e, b=b, negc=negc: e.dma_start(
                out=negc[:],
                in_=s_ng[b].rearrange("h -> () h").broadcast_to([128, H])),
              r=[f"s_ng{b}", "negc"], w=["negc"], tag="p3.ldng")
        for kt in range(NT):
            P.dma("cld", lambda e, kt=kt, b=b, vres=vres: e.dma_start(
                    out=vres[kt][:], in_=s_v[b, kt]),
                  r=[f"s_v{b}.{kt}", f"vres{kt}"], w=[f"vres{kt}"],
                  tag=f"p3.ldv.{kt}")
        for kt in range(NT):
            P.v(lambda e, kt=kt, vaug=vaug: e.memset(
                    vaug[:, kt * 65 + 64:kt * 65 + 65], 1.0),
                r=["vaug"], w=["vaug"], tag="p3.vones")
        for p in range(8):
            qb, kb = qpair[p % 2], kpair[p % 2]
            kqb, kkb = f"q3p{p % 2}", f"k3p{p % 2}"
            P.dma("cld", lambda e, p=p, qb=qb, b=b: e.dma_start(
                    out=qb[:], in_=s_qp[b, p]),
                  r=[f"s_qp{b}.{p}", kqb], w=[kqb], tag=f"p3.ldq.{p}")
            P.dma("cld", lambda e, p=p, kb=kb, b=b: e.dma_start(
                    out=kb[:], in_=s_kp[b, p]),
                  r=[f"s_kp{b}.{p}", kkb], w=[kkb], tag=f"p3.ldk.{p}")
            for half in range(2):
                h = 2 * p + half
                base = 64 * half
                for kt in range(NT):
                    P.g(lambda e, kt=kt, h=h, vres=vres, vaug=vaug:
                        e.tensor_copy(
                            out=vaug[:, kt * 65:kt * 65 + 64],
                            in_=vres[kt][:, h * HD:(h + 1) * HD]),
                        r=[f"vres{kt}", "vaug"], w=["vaug"],
                        tag=f"p3.va.{kt}")
                SLOTS = [(pA, 0, KA0), (pA, 512, KA1),
                         (pB, 0, KB0), (pB, 512, KB1)]
                chunks = {}
                for kt in range(NT):
                    lst = []
                    c0 = 128 * kt
                    while c0 < S:
                        c1 = min((c0 // 512 + 1) * 512, S)
                        lst.append((c0, c1))
                        c0 = c1
                    chunks[kt] = lst
                scnt = [0]

                def emit_qk(kt, h=h, base=base, qb=qb, kb=kb, kqb=kqb,
                            kkb=kkb, negc=negc):
                    pt_t = ptil[kt % 2]
                    kpt = f"ptil{kt % 2}"
                    for (c0, c1) in chunks[kt]:
                        pp, off, kps = SLOTS[scnt[0] % 4]
                        scnt[0] += 1
                        P.t(lambda e, kt=kt, c0=c0, c1=c1, pp=pp, off=off,
                            base=base, qb=qb, kb=kb: e.matmul(
                                pp[:, off:off + (c1 - c0)],
                                kb[base:base + 64, 128 * kt:128 * (kt + 1)],
                                qb[base:base + 64, c0:c1],
                                start=True, stop=True),
                            r=[kqb, kkb, kps], w=[kps],
                            tag=f"p3B.{h}.{kt}")
                        P.s(lambda e, c0=c0, c1=c1, pp=pp, off=off, h=h,
                            pt_t=pt_t, negc=negc: e.activation(
                                out=pt_t[:, c0:c1],
                                in_=pp[:, off:off + (c1 - c0)],
                                func=AF.Exp, scale=0.125,
                                bias=negc[:, h:h + 1]),
                            r=[kps, "negc", kpt], w=[kpt, kps],
                            tag=f"p3E.{h}.{kt}")
                    q0 = 128 * kt
                    P.v(lambda e, q0=q0, pt_t=pt_t: e.tensor_tensor(
                            out=pt_t[:, q0:q0 + 128],
                            in0=pt_t[:, q0:q0 + 128], in1=ltm_b[:],
                            op=ALU.mult),
                        r=[kpt, "ltm_b"], w=[kpt], tag=f"p3M.{h}.{kt}")

                def emit_av(kt, h=h, vaug=vaug):
                    pt_t = ptil[kt % 2]
                    kpt = f"ptil{kt % 2}"
                    for (c0, c1) in chunks[kt]:
                        P.t(lambda e, kt=kt, c0=c0, c1=c1, pt_t=pt_t,
                            vaug=vaug: e.matmul(
                                pAV[:, c0:c1],
                                vaug[:, kt * 65:(kt + 1) * 65],
                                pt_t[:, c0:c1], start=(kt == 0),
                                stop=(kt == NT - 1
                                      or c1 <= 128 * (kt + 1))),
                            r=["vaug", kpt, "pAV"], w=["pAV"],
                            tag=f"p3AV.{h}.{kt}")

                emit_qk(0)
                for kt in range(1, NT):
                    emit_qk(kt)
                    emit_av(kt - 1)
                emit_av(NT - 1)
                for c in range(0, S, 512):
                    P.s(lambda e, c=c, avs=avs: e.activation(
                            out=avs[:, c:c + 512], in_=pAV[:, c:c + 512],
                            func=AF.Copy),
                        r=["pAV", "avs"], w=["avs", "pAV"],
                        tag=f"p3.evav.{h}")
                for j in range(NT):
                    pp1, off1, k1 = SLOTS[(2 * j) % 4]
                    pp2, off2, k2 = SLOTS[(2 * j + 1) % 4]
                    jj = j % 2
                    krc, kot = f"recl{jj}", f"otm{jj}"
                    P.t(lambda e, j=j, avs=avs, pp1=pp1, off1=off1:
                        e.transpose(
                            pp1[:, off1:off1 + 65],
                            avs[:, 128 * j:128 * (j + 1)],
                            ident_f[0:65, 0:65]),
                        r=["avs", "ident_f", k1], w=[k1],
                        tag=f"p3.tr1.{h}")
                    P.v(lambda e, recl=recl, jj=jj, pp1=pp1, off1=off1:
                        e.reciprocal(
                            out=recl[:, jj:jj + 1],
                            in_=pp1[:, off1 + 64:off1 + 65]),
                        r=[k1, krc], w=[krc], tag=f"p3.rc.{h}")
                    P.v(lambda e, otm=otm, recl=recl, jj=jj, pp1=pp1,
                        off1=off1: e.tensor_tensor(
                            out=otm[:, 64 * jj:64 * jj + 64],
                            in0=pp1[:, off1:off1 + 64],
                            in1=recl[:, jj:jj + 1].broadcast_to([128, 64]),
                            op=ALU.mult),
                        r=[k1, krc, kot], w=[kot, k1],
                        tag=f"p3.dv.{h}")
                    P.t(lambda e, otm=otm, jj=jj, pp2=pp2, off2=off2:
                        e.transpose(
                            pp2[0:64, off2:off2 + 128],
                            otm[:, 64 * jj:64 * jj + 64], ident_f[:]),
                        r=[kot, "ident_f", k2], w=[k2],
                        tag=f"p3.tr2.{h}")
                    P.v(lambda e, j=j, athd=athd, pp2=pp2, off2=off2:
                        e.tensor_copy(
                            out=athd[:, 128 * j:128 * (j + 1)],
                            in_=pp2[0:64, off2:off2 + 128]),
                        r=[k2, "athd"], w=["athd", k2], tag=f"p3.ev2.{h}")
                P.dma("csp", lambda e, h=h, b=b, athd=athd: e.dma_start(
                        out=s_at[b, h], in_=athd[:]),
                      r=["athd"], w=[f"s_at{b}.{h}"], tag=f"p3.spat.{h}")

    PH = os.environ.get("MK_PHASES", "1234")
    NSEQ = int(os.environ.get("MK_NSEQ", str(BPC)))
    for b in range(NSEQ):
        T = {}
        T["xnT"] = [mk.sb(f"xnT{c}_{b}", [128, S], BF16) for c in range(EC)]
        p1names = [f"xt0_{b}", f"xt1_{b}", f"xnf_{b}", f"xqf_{b}",
                   f"lnsc_{b}", f"xqTall_{b}"]
        T["xt"] = [mk.sb(p1names[0], [128, E], F32),
                   mk.sb(p1names[1], [128, E], F32)]
        T["xnf"] = mk.sb(p1names[2], [128, E], F32)
        T["xqf"] = mk.sb(p1names[3], [128, E], F32)
        T["ln_sc"] = mk.sb(p1names[4], [128, 64], F32)
        T["xqTall"] = mk.sb(p1names[5], [128, NT * E], BF16)
        q1set, q1names = alloc_qset(f"q1{b}", 1024)
        mk.q = q1set
        P.phase = f"p1.{b}"
        phase_p1(b, T)
        mk.free(p1names + q1names)
        if "2" not in PH:
            mk.free([f"xnT{c}_{b}" for c in range(EC)])
            continue
        p2names = [f"cqT_{b}", f"ckvT_{b}", f"qrp_{b}", f"krp_{b}",
                   f"rpre_{b}", f"qpair0_{b}", f"qpair1_{b}", f"kpair0_{b}",
                   f"kpair1_{b}", f"nstat_{b}", f"nrow_{b}", f"ngrow_{b}",
                   f"vevac_{b}", f"vstg0_{b}", f"vstg1_{b}", f"vstg2_{b}",
                   f"vstg3_{b}"]
        T["cqT"] = mk.sb(p2names[0], [128, 2 * S], BF16)
        T["ckvT"] = mk.sb(p2names[1], [128, 2 * S], BF16)
        T["qrp"] = mk.sb(p2names[2], [128, S], F32)
        T["krp"] = mk.sb(p2names[3], [128, S], F32)
        T["rpre"] = mk.sb(p2names[4], [128, S], F32)
        T["qpair"] = [mk.sb(p2names[5], [128, S], BF16),
                      mk.sb(p2names[6], [128, S], BF16)]
        T["kpair"] = [mk.sb(p2names[7], [128, S], BF16),
                      mk.sb(p2names[8], [128, S], BF16)]
        T["nstat"] = mk.sb(p2names[9], [128, 2], F32)
        T["nrow"] = mk.sb(p2names[10], [1, 128], F32)
        T["ngrow"] = mk.sb(p2names[11], [1, H], F32)
        T["vevac"] = mk.sb(p2names[12], [128, 512], F32)
        T["vstg"] = [mk.sb(p2names[13 + j], [128, E], BF16) for j in range(4)]
        P.phase = f"p2.{b}"
        phase_p2(b, T)
        mk.free(p2names + [f"xnT{c}_{b}" for c in range(EC)])
        if "3" not in PH:
            continue
        p3names = ([f"vres{kt}_{b}" for kt in range(NT)] +
                   [f"vaug_{b}", f"ptil0_{b}", f"ptil1_{b}", f"avs_{b}",
                    f"otm_{b}", f"recl_{b}", f"athd_{b}", f"negc_{b}",
                    f"q3p0_{b}", f"q3p1_{b}", f"k3p0_{b}", f"k3p1_{b}"])
        T["vres"] = [mk.sb(f"vres{kt}_{b}", [128, E], BF16)
                     for kt in range(NT)]
        T["vaug"] = mk.sb(f"vaug_{b}", [128, NT * 65], BF16)
        T["ptil"] = [mk.sb(f"ptil0_{b}", [128, S], BF16),
                     mk.sb(f"ptil1_{b}", [128, S], BF16)]
        T["avs"] = mk.sb(f"avs_{b}", [65, S], F32)
        T["otm"] = mk.sb(f"otm_{b}", [128, 128], F32)
        T["recl"] = mk.sb(f"recl_{b}", [128, 2], F32)
        T["athd"] = mk.sb(f"athd_{b}", [HD, S], BF16)
        T["negc"] = mk.sb(f"negc_{b}", [128, H], F32)
        T["qpair3"] = [mk.sb(f"q3p0_{b}", [128, S], BF16),
                       mk.sb(f"q3p1_{b}", [128, S], BF16)]
        T["kpair3"] = [mk.sb(f"k3p0_{b}", [128, S], BF16),
                       mk.sb(f"k3p1_{b}", [128, S], BF16)]
        P.phase = f"p3.{b}"
        phase_p3(b, T)
        mk.free(p3names)

    mk.free(["cosT", "sinT", "stgw"] + [nm + "_s" for nm in wspec] +
            [nm + "_s" for nm in ("bq", "bqc", "bqr", "bkv", "bkr", "bkc",
                                  "bv")] + ["pA", "pB", "pAV"])

    GT = 1
    xqg = [mk.sb(f"xqg{i}", [128, GT * E], BF16) for i in range(2)]
    atg = [mk.sb(f"atg{i}", [128, H * GT * 64], BF16) for i in range(2)]
    wgq = mk.sb("wgq", [128, EC * I], BF16)
    wuq = mk.sb("wuq", [128, EC * I], BF16)
    wdq = mk.sb("wdq", [128, IC * E], BF16)
    sg_t = [mk.sb("sg_t0", [128, 512], F32), mk.sb("sg_t1", [128, 512], F32)]
    h_t = [mk.sb("h_t0", [128, 512], F32), mk.sb("h_t1", [128, 512], F32)]
    hqf = [mk.sb("hqf0", [128, 512], BF16), mk.sb("hqf1", [128, 512], BF16)]
    hqT = [mk.sb("hqTa", [128, IC * 128], BF16),
           mk.sb("hqTb", [128, IC * 128], BF16)]
    xre = [mk.sb("xre0", [128, E], F32), mk.sb("xre1", [128, E], F32)]
    q4 = [alloc_qset("q40", 512), alloc_qset("q41", 512)]
    psG = [mk.ps("psG0", [128, 512], F32), mk.ps("psG1", [128, 512], F32)]
    psU = [mk.ps("psU0", [128, 512], F32), mk.ps("psU1", [128, 512], F32)]
    psD = mk.ps("psD", [128, E], F32)
    psT4 = [mk.ps("psT4a", [128, 1024], BF16),
            mk.ps("psT4b", [128, 1024], BF16)]

    def phase_p4():
        pend = []
        for c in range(EC):
            P.dma("cw", lambda e, c=c: e.dma_start(
                    out=wgq[:, c * I:(c + 1) * I],
                    in_=s_wg[128 * c:128 * (c + 1), :]),
                  r=["scr_wg", "wgq"], w=["wgq"], tag="p4.lwg")
            P.dma("cw", lambda e, c=c: e.dma_start(
                    out=wuq[:, c * I:(c + 1) * I],
                    in_=s_wu[128 * c:128 * (c + 1), :]),
                  r=["scr_wu", "wuq"], w=["wuq"], tag="p4.lwu")
        for c in range(IC):
            P.dma("cw", lambda e, c=c: e.dma_start(
                    out=wdq[:, c * E:(c + 1) * E],
                    in_=s_wd[128 * c:128 * (c + 1), :]),
                  r=["scr_wd", "wdq"], w=["wdq"], tag="p4.lwd")
        NTOT = BPC * NT
        CH = (I + 511) // 512
        cdefs0 = []
        for c0 in range(0, I, 512):
            cw = min(512, I - c0)
            cdefs0.append((c0, cw, (c0 // 512) % 2))

        def emit_load(t_idx):
            b, grp = divmod(t_idx, NT)
            gbuf, kg = xqg[t_idx % 2], f"xqg{t_idx % 2}"
            P.dma("cld", lambda e, b=b, grp=grp, gbuf=gbuf: e.dma_start(
                    out=gbuf[:], in_=s_xqT[b][:, grp * E:(grp + 1) * E]),
                  r=[f"s_xqT{b}", kg], w=[kg], tag=f"p4.lxq.{b}.{grp}")

        def emit_load2(t_idx):
            b, grp = divmod(t_idx, NT)
            abuf, ka = atg[t_idx % 2], f"atg{t_idx % 2}"
            for hh in range(2):
                P.dma("cld", lambda e, b=b, grp=grp, abuf=abuf, hh=hh:
                      e.dma_start(
                          out=abuf[64 * hh:64 * hh + 64, :].rearrange(
                              "d (c t) -> d c t", t=128),
                          in_=s_at[b][:, :, 128 * grp:128 * (grp + 1)]
                          .rearrange("(c hh) d t -> hh d c t", hh=2)[hh]),
                      r=[f"s_at{b}.{h}" for h in range(H)] + [ka],
                      w=[ka], tag=f"p4.lat.{b}.{grp}")
            P.dma("cin", lambda e, b=b, grp=grp, t_idx=t_idx: e.dma_start(
                    out=xre[t_idx % 2][:],
                    in_=d_x[b * S + 128 * grp:b * S + 128 * (grp + 1), :]),
                  r=[f"xre{t_idx % 2}"], w=[f"xre{t_idx % 2}"],
                  tag=f"p4.lx.{b}.{grp}")

        def emit_gu(t_idx, k):
            c0, cw, ci = cdefs0[k]
            gbuf, kg = xqg[t_idx % 2], f"xqg{t_idx % 2}"
            kG, kU = f"psG{ci}", f"psU{ci}"
            pG, pU = psG[ci], psU[ci]
            for kc in range(EC):
                P.t(lambda e, kc=kc, c0=c0, cw=cw, gbuf=gbuf, pG=pG:
                    e.matmul(
                        pG[:, 0:cw],
                        gbuf[:, kc * 128:(kc + 1) * 128],
                        wgq[:, kc * I + c0:kc * I + c0 + cw],
                        start=(kc == 0), stop=(kc == EC - 1)),
                    r=[kg, "wgq", kG], w=[kG], tag="p4.g")
                P.t(lambda e, kc=kc, c0=c0, cw=cw, gbuf=gbuf, pU=pU:
                    e.matmul(
                        pU[:, 0:cw],
                        gbuf[:, kc * 128:(kc + 1) * 128],
                        wuq[:, kc * I + c0:kc * I + c0 + cw],
                        start=(kc == 0), stop=(kc == EC - 1)),
                    r=[kg, "wuq", kU], w=[kU], tag="p4.u")
            P.s(lambda e, cw=cw, pG=pG, ci=ci: e.activation(
                    out=sg_t[ci][:, 0:cw], in_=pG[:, 0:cw], func=AF.Silu),
                r=[kG, f"sg{ci}"], w=[f"sg{ci}", kG], tag="p4.si")
            P.v(lambda e, cw=cw, pU=pU, ci=ci: e.tensor_tensor(
                    out=h_t[ci][:, 0:cw], in0=sg_t[ci][:, 0:cw],
                    in1=pU[:, 0:cw], op=ALU.mult),
                r=[f"sg{ci}", kU, f"h{ci}"], w=[f"h{ci}", kU], tag="p4.h")

        def emit_qtr(t_idx, k):
            c0, cw, ci = cdefs0[k]
            kq = f"hqT{t_idx % 2}"
            hq_b = hqT[t_idx % 2]
            if k == 0:
                emit_load2(t_idx)
            mk.emit_quant(h_t[ci][:, 0:cw], hqf[ci][:, 0:cw], cw,
                          [f"h{ci}"], f"hq{ci}", qd=q4[ci][0])
            for cc in range(cw // 128):
                c = (c0 + cc * 128) // 128
                tj = cc % 2
                kT = f"psT4{tj}"
                pT = psT4[tj]
                P.t(lambda e, cc=cc, pT=pT, ci=ci: e.transpose(
                        pT[:, 0:128],
                        hqf[ci][:, cc * 128:(cc + 1) * 128],
                        ident_b[:]),
                    r=[f"hq{ci}", "ident_b", kT], w=[kT], tag="p4.trh")
                P.s(lambda e, c=c, pT=pT, hq_b=hq_b: e.activation(
                        out=hq_b[:, c * 128:(c + 1) * 128],
                        in_=pT[:, 0:128], func=AF.Copy),
                    r=[kT, kq], w=[kq, kT], tag="p4.evh")

        def emit_tail(t_idx):
            b, grp = divmod(t_idx, NT)
            abuf, ka = atg[t_idx % 2], f"atg{t_idx % 2}"
            kq = f"hqT{t_idx % 2}"
            hq_b = hqT[t_idx % 2]
            for halfc in range(2):
                cl, cr = halfc * 512, halfc * 512 + 512
                for kc in range(IC):
                    P.t(lambda e, kc=kc, cl=cl, cr=cr, hq_b=hq_b: e.matmul(
                            psD[:, cl:cr],
                            hq_b[:, kc * 128:(kc + 1) * 128],
                            wdq[:, kc * E + cl:kc * E + cr],
                            start=(kc == 0), stop=False),
                        r=[kq, "wdq", "psD"], w=["psD"], tag="p4.d")
                for kc in range(EC):
                    P.t(lambda e, kc=kc, cl=cl, cr=cr, abuf=abuf: e.matmul(
                            psD[:, cl:cr],
                            abuf[:, kc * 128:(kc + 1) * 128],
                            w_sb["wo"][:, kc * E + cl:kc * E + cr],
                            start=False, stop=False),
                        r=[ka, "wo_s", "psD"], w=["psD"], tag="p4.wo")
                P.t(lambda e, cl=cl, cr=cr: e.matmul(
                        psD[:, cl:cr], ones_bf[:, 0:128],
                        bo_row[:, cl:cr], start=False, stop=True),
                    r=["ones_bf", "bo_row", "psD"], w=["psD"], tag="p4.bo")
            xr, kxr = xre[t_idx % 2], f"xre{t_idx % 2}"
            P.v(lambda e, xr=xr: e.tensor_tensor(
                    out=xr[:], in0=xr[:], in1=psD[:], op=ALU.add),
                r=[kxr, "psD"], w=[kxr, "psD"], tag="p4.cmb")
            P.dma("cout", lambda e, b=b, grp=grp, xr=xr: e.dma_start(
                    out=d_y[b * S + 128 * grp:b * S + 128 * (grp + 1), :],
                    in_=xr[:]),
                  r=[kxr], w=[kxr, "y_out"], tag=f"p4.st.{b}.{grp}")

        emitted = 0
        for g in range(NTOT * CH + 2):
            while emitted <= min(g // CH + 1, NTOT - 1):
                emit_load(emitted)
                emitted += 1
            if g >= 2:
                t_idx, k = divmod(g - 2, CH)
                emit_qtr(t_idx, k)
                if k == CH - 1:
                    emit_tail(t_idx)
            if g < NTOT * CH:
                t_idx, k = divmod(g, CH)
                emit_gu(t_idx, k)

    if "4" in PH:
        P.phase = "p4"
        phase_p4()
        P.y(lambda e: e.nop(), r=["y_out"], tag="final.wait")
    else:
        P.set_barrier()
    table = P.finalize_and_emit(probe=probe, piece_table=piece_table)
    mk.close_all()
    if probe:
        return nc, table
    return nc


def build_program2():
    _, table = build_program(probe=True)
    return build_program(piece_table=table)


def _host_tables():
    inv_freq = 1.0 / 10000.0 ** (np.arange(0, DR, 2, dtype=np.float64) / DR)
    t = np.arange(S, dtype=np.float64)
    freqs = t[:, None] * inv_freq[None, :]
    cos64 = np.cos(freqs).astype(np.float32).T
    sin64 = np.sin(freqs).astype(np.float32).T
    cos = np.ascontiguousarray(np.concatenate([cos64, cos64], axis=0))
    sin = np.ascontiguousarray(np.concatenate([sin64, sin64], axis=0))
    ident = np.eye(128, dtype=np.float32)
    ltm = np.ascontiguousarray(np.triu(np.ones((128, 128), dtype=np.float32)))
    return cos, sin, ident, ltm


_NC_CACHE = {}


def kernel(**inputs):
    if "prog" not in _NC_CACHE:
        _NC_CACHE["prog"] = build_program2()
    nc = _NC_CACHE["prog"]
    cos, sin, ident, ltm = _host_tables()
    x = np.ascontiguousarray(np.asarray(inputs["L0_attn_x"], dtype=np.float32))
    common = {
        "wq": inputs["L0_attn_wq"], "bq": inputs["L0_attn_bq"],
        "wqc": inputs["L0_attn_wqc"], "bqc": inputs["L0_attn_bqc"],
        "wqr": inputs["L0_attn_wqr"], "bqr": inputs["L0_attn_bqr"],
        "wkv": inputs["L0_attn_wkv"], "bkv": inputs["L0_attn_bkv"],
        "wkr": inputs["L0_attn_wkr"], "bkr": inputs["L0_attn_bkr"],
        "wkc": inputs["L0_attn_wkc"], "bkc": inputs["L0_attn_bkc"],
        "wv": inputs["L0_attn_wv"], "bv": inputs["L0_attn_bv"],
        "wo": inputs["L0_attn_wo"], "bo": inputs["L0_attn_bo"],
        "wg": inputs["L0_ffn_w_gate"], "wu": inputs["L0_ffn_w_up"],
        "wd": inputs["L0_ffn_w_down"],
        "costab": cos, "sintab": sin, "identity": ident, "ltmask": ltm,
    }
    common = {k: np.ascontiguousarray(np.asarray(v, dtype=np.float32))
              for k, v in common.items()}
    in_maps = []
    for c in range(NCORES):
        m = dict(common)
        m["x"] = np.ascontiguousarray(
            x[c * BPC:(c + 1) * BPC].reshape(BPC * S, E))
        in_maps.append(m)
    res = run_bass_kernel_spmd(nc, in_maps, core_ids=list(range(NCORES)))
    outs = [np.asarray(r["y"]).reshape(BPC, S, E) for r in res.results]
    return np.concatenate(outs, axis=0)


if __name__ == "__main__":
    import time
    t0 = time.time()
    nc = build_program()
    print("built in %.1fs, instructions: %d" % (time.time() - t0,
                                                nc.n_instructions()))


# version 21
# speedup vs baseline: 1.0109x; 1.0109x over previous
import os
import numpy as np
import concourse.bass as bass
import concourse.mybir as mybir
from concourse.bass_utils import run_bass_kernel_spmd

F32 = mybir.dt.float32
BF16 = mybir.dt.bfloat16
AF = mybir.ActivationFunctionType
ALU = mybir.AluOpType
AX = mybir.AxisListType

NCORES = 8
B = 16
BPC = B // NCORES
S = 2048
E = 1024
H = 16
HD = 64
DC = 256
DR = 128
I = 2816
NT = S // 128
EC = E // 128
IC = I // 128


class _Step:
    __slots__ = ("eng", "emit", "dma_ch", "deps", "needs_inc", "inc_idx",
                 "ch_idx", "tag", "phase", "raw_ids", "pos")

    def __init__(self, eng, emit, dma_ch, tag):
        self.eng = eng
        self.emit = emit
        self.dma_ch = dma_ch
        self.deps = []
        self.needs_inc = False
        self.inc_idx = None
        self.ch_idx = None
        self.tag = tag


class Prog:
    ENGINES = ("tensor", "vector", "scalar", "gpsimd", "sync")

    def __init__(self, nc):
        self.nc = nc
        self.steps = []
        self.last_write = {}
        self.readers = {}
        self.channels = {}
        self.ch_counts = {}
        self.barrier = None
        self.phase = "init"
        self._eng_pos = {}

    def channel(self, name, eng, pool=1):
        for k in range(pool):
            self.channels[f"{name}#{k}"] = eng
            self.ch_counts[f"{name}#{k}"] = 0
        self._pools = getattr(self, "_pools", {})
        self._pools[name] = [pool, 0]

    def add(self, eng, emit, r=(), w=(), dma_ch=None, tag=""):
        s = _Step(eng, emit, dma_ch, tag)
        s.phase = self.phase
        deps = {}
        raw = set()
        for bk in r:
            lw = self.last_write.get(bk)
            if lw is not None:
                deps[id(lw)] = lw
                raw.add(id(lw))
            self.readers.setdefault(bk, []).append(s)
        for bk in w:
            lw = self.last_write.get(bk)
            if lw is not None:
                deps[id(lw)] = lw
            for rd in self.readers.get(bk, ()):
                deps[id(rd)] = rd
            self.last_write[bk] = s
            self.readers[bk] = []
        if self.barrier is not None:
            deps[id(self.barrier)] = self.barrier
        deps.pop(id(s), None)
        s.deps = list(deps.values())
        s.raw_ids = raw
        s.pos = self._eng_pos.get(eng, 0)
        self._eng_pos[eng] = s.pos + 1
        if dma_ch is not None:
            self.ch_counts[dma_ch] += 1
            s.ch_idx = self.ch_counts[dma_ch]
        self.steps.append(s)
        return s

    def t(self, emit, r=(), w=(), tag=""):
        return self.add("tensor", emit, r, w, tag=tag)

    def v(self, emit, r=(), w=(), tag=""):
        return self.add("vector", emit, r, w, tag=tag)

    def s(self, emit, r=(), w=(), tag=""):
        return self.add("scalar", emit, r, w, tag=tag)

    def g(self, emit, r=(), w=(), tag=""):
        return self.add("gpsimd", emit, r, w, tag=tag)

    def y(self, emit, r=(), w=(), tag=""):
        return self.add("sync", emit, r, w, tag=tag)

    def dma(self, ch, emit, r=(), w=(), tag=""):
        pool, nxt = self._pools[ch]
        self._pools[ch][1] = (nxt + 1) % pool
        sub = f"{ch}#{nxt}"
        return self.add(self.channels[sub], emit, r, w, dma_ch=sub, tag=tag)

    def set_barrier(self):
        allk = list(self.last_write.keys())
        old = self.barrier
        self.barrier = None
        b = self.add("sync", lambda e: e.nop(), r=allk, tag="barrier")
        if old is not None:
            b.deps.append(old)
        self.barrier = b

    def finalize_and_emit(self, probe=False, piece_table=None):
        nc = self.nc
        strict = os.environ.get("MK_STRICT", "0") == "1"
        ss = set((os.environ.get("MK_SS", "vector,scalar,gpsimd")).split(",")) - {""}
        self._ss = ss
        self._knear = int(os.environ.get("MK_K", "3"))

        def near_raw(s, d):
            return (d.eng == s.eng and d.eng in ss and id(d) in s.raw_ids
                    and s.pos - d.pos <= self._knear)
        self._near_raw = near_raw
        self.prefix = {}
        if piece_table is not None:
            run = {}
            for (ch, idx), pieces in sorted(piece_table.items()):
                run[ch] = run.get(ch, 0) + pieces
                self.prefix[(ch, idx)] = run[ch]
        for s in self.steps:
            for d in s.deps:
                if d.dma_ch is None and (strict or d.eng != s.eng
                                         or near_raw(s, d)):
                    d.needs_inc = True
        inc_counters = {e: 0 for e in self.ENGINES}
        for s in self.steps:
            if s.dma_ch is None and s.needs_inc:
                inc_counters[s.eng] += 1
                s.inc_idx = inc_counters[s.eng]
        self._slack = int(os.environ.get("MK_SLACK", "0"))
        self._totals = dict(inc_counters)
        self.eng_sems = {}
        self._sem_guards = []
        for e in self.ENGINES:
            if inc_counters[e]:
                g = nc.semaphore("sem_" + e)
                self.eng_sems[e] = g.__enter__()
                self._sem_guards.append(g)
        self.ch_sems = {}
        for chn in self.channels:
            if self.ch_counts[chn]:
                g = nc.semaphore("sem_" + chn)
                self.ch_sems[chn] = g.__enter__()
                self._sem_guards.append(g)
        if probe:
            g = nc.semaphore("sem_sentinel")
            self.sentinel = g.__enter__()
            self._sem_guards.append(g)
        per_eng = {e: [] for e in self.ENGINES}
        for s in self.steps:
            per_eng[s.eng].append(s)

        def make_body(steps):
            def body(eng):
                satisfied = {}

                def wait(sem, val):
                    if satisfied.get(id(sem), -1) >= val:
                        return
                    eng.wait_ge(sem, val)
                    satisfied[id(sem)] = val

                import os as _os
                dbg = _os.environ.get("MK_TRACE") == "1"
                prev_inc = [None]
                def chval(ch, idx):
                    if idx <= 0:
                        return 0
                    return 16 * self.prefix.get((ch, idx), idx)

                for s in steps:
                    for d in s.deps:
                        if d.dma_ch is not None:
                            if dbg:
                                print(f"[{s.eng}] wait {d.dma_ch} >= {chval(d.dma_ch, d.ch_idx)}")
                            wait(self.ch_sems[d.dma_ch], chval(d.dma_ch, d.ch_idx))
                        elif d.eng != s.eng:
                            if dbg:
                                print(f"[{s.eng}] wait {d.eng} >= {d.inc_idx} (dep {d.tag})")
                            wait(self.eng_sems[d.eng],
                                 min(d.inc_idx + self._slack,
                                     self._totals[d.eng]))
                        elif ((strict or self._near_raw(s, d))
                              and d.inc_idx is not None):
                            wait(self.eng_sems[d.eng], d.inc_idx)
                    if s.dma_ch is not None and s.ch_idx > 1:
                        wait(self.ch_sems[s.dma_ch], chval(s.dma_ch, s.ch_idx - 1))
                    with nc.named_scope(s.phase):
                        inst = s.emit(eng)
                    if s.dma_ch is not None:
                        if dbg:
                            print(f"[{s.eng}] DMA {s.tag} inc {s.dma_ch}")
                        inst.then_inc(self.ch_sems[s.dma_ch], 16)
                        if probe:
                            eng.sem_inc(self.sentinel, 1)
                    elif s.needs_inc:
                        if dbg:
                            print(f"[{s.eng}] OP {s.tag} inc -> {s.inc_idx}")
                        inst.then_inc(self.eng_sems[s.eng], 1)
                    elif dbg:
                        print(f"[{s.eng}] OP {s.tag}")
            return body

        with nc.Block() as block:
            for e in self.ENGINES:
                if per_eng[e]:
                    getattr(block, e)(make_body(per_eng[e]))
        table = None
        if probe:
            table = {}
            dma_steps_by_eng = {e: [st for st in per_eng[e] if st.dma_ch]
                                for e in self.ENGINES}
            cursor = {e: 0 for e in self.ENGINES}
            pending = {}
            eng_updates = {}
            for inst in nc.all_instructions():
                si = getattr(inst, "sync_info", None)
                if si is None:
                    continue
                for u in si.on_update:
                    nm = u.ant_name or ""
                    if nm.startswith("sem_") and "#" in nm:
                        pending[nm[4:]] = pending.get(nm[4:], 0) + 1
                    elif nm == "sem_sentinel":
                        done = False
                        for e in self.ENGINES:
                            lst = dma_steps_by_eng[e]
                            k = cursor[e]
                            if k < len(lst) and pending.get(lst[k].dma_ch, 0) > 0:
                                st = lst[k]
                                table[(st.dma_ch, st.ch_idx)] = pending.pop(st.dma_ch)
                                cursor[e] += 1
                                done = True
                                break
                        assert done, f"sentinel attribution failed {pending}"
                    elif nm.startswith("sem_"):
                        eng_updates[nm] = eng_updates.get(nm, 0) + 1
            for e in self.ENGINES:
                exp = sum(1 for st in per_eng[e]
                          if st.dma_ch is None and st.needs_inc)
                got = eng_updates.get("sem_" + e, 0)
                assert got == exp, f"engine sem count mismatch {e}: {got} != {exp}"
        for g in self._sem_guards:
            g.__exit__(None, None, None)
        return table


class MK:
    def __init__(self, nc):
        self.nc = nc
        self.P = Prog(nc)
        self.P.channel("cin", "sync", pool=2)
        self.P.channel("cw", "sync", pool=2)
        self.P.channel("csp", "gpsimd", pool=3)
        self.P.channel("cld", "scalar", pool=3)
        self.P.channel("cout", "gpsimd", pool=2)
        self._guards = []

    def sb(self, name, shape, dtype):
        g = self.nc.sbuf_tensor(name, shape, dtype)
        t = g.__enter__()
        self._guards.append((name, g))
        return t

    def ps(self, name, shape, dtype=F32):
        g = self.nc.psum_tensor(name, shape, dtype)
        t = g.__enter__()
        self._guards.append((name, g))
        return t

    def free(self, names):
        self.P.set_barrier()
        nameset = set(names)
        keep = []
        for nm, g in reversed(self._guards):
            if nm in nameset:
                g.__exit__(None, None, None)
                nameset.discard(nm)
            else:
                keep.append((nm, g))
        assert not nameset, f"not found: {nameset}"
        self._guards = list(reversed(keep))

    def close_all(self):
        for nm, g in reversed(self._guards):
            g.__exit__(None, None, None)
        self._guards = []

    def emit_quant(self, x_ap, out_ap, F, in_keys, out_key, qd=None):
        P = self.P
        q = qd if qd is not None else self.q
        pfx = q.get("pfx", "q")
        G = F // 16
        mx, s_, r_, nsp = q["mx"], q["s"], q["r"], q["nsp"]
        t_, ca, cb, cc, c1, sgn, sq2 = (q["t"], q["ca"], q["cb"], q["cc"],
                                        q["c1"], q["sgn"], q["sq"])
        kx = list(in_keys)
        x3 = x_ap.rearrange("p (g k) -> p g k", k=16)
        P.v(lambda e: e.tensor_reduce(out=mx[:, :G], in_=x3, axis=AX.X,
                                      op=ALU.max, apply_absolute_value=True),
            r=kx + [pfx + ".mx"], w=[pfx + ".mx"], tag=pfx + ".mx")
        P.v(lambda e: e.tensor_scalar(out=s_[:, :G], in0=mx[:, :G],
                                      scalar1=1.0 / 6.0, scalar2=1e-12,
                                      op0=ALU.mult, op1=ALU.max),
            r=[pfx + ".mx", pfx + ".s"], w=[pfx + ".s"], tag=pfx + ".s")
        P.v(lambda e: e.reciprocal(out=r_[:, :G], in_=s_[:, :G]),
            r=[pfx + ".s", pfx + ".r"], w=[pfx + ".r"], tag=pfx + ".r")
        P.v(lambda e: e.tensor_scalar_mul(out=nsp[:, :G], in0=s_[:, :G],
                                          scalar1=0.5),
            r=[pfx + ".s", pfx + ".nsp"], w=[pfx + ".nsp"], tag=pfx + ".nsp")
        P.s(lambda e: e.activation(out=sgn[:, :F], in_=x_ap, func=AF.Sign),
            r=kx + [pfx + ".sgn"], w=[pfx + ".sgn"], tag=pfx + ".sgn")
        P.s(lambda e: e.activation(out=t_[:, :F], in_=x_ap, func=AF.Square),
            r=kx + [pfx + ".t"], w=[pfx + ".t"], tag=pfx + ".x2")
        P.s(lambda e: e.activation(out=s_[:, :G], in_=r_[:, :G],
                                   func=AF.Square),
            r=[pfx + ".r", pfx + ".s"], w=[pfx + ".s"], tag=pfx + ".r2")
        r2b = s_[:, :G].unsqueeze(2).broadcast_to([128, G, 16])
        P.v(lambda e: e.tensor_tensor(
                out=sq2[:, :F].rearrange("p (g k) -> p g k", k=16),
                in0=t_[:, :F].rearrange("p (g k) -> p g k", k=16),
                in1=r2b, op=ALU.mult),
            r=[pfx + ".t", pfx + ".s", pfx + ".sq"], w=[pfx + ".sq"], tag=pfx + ".sq")
        P.v(lambda e: e.tensor_scalar(out=ca[:, :F], in0=sq2[:, :F],
                                      scalar1=0.0625, scalar2=None,
                                      op0=ALU.is_gt),
            r=[pfx + ".sq", pfx + ".ca"], w=[pfx + ".ca"], tag=pfx + ".c")
        P.v(lambda e: e.tensor_scalar(out=c1[:, :F], in0=sq2[:, :F],
                                      scalar1=0.5625, scalar2=None,
                                      op0=ALU.is_gt),
            r=[pfx + ".sq", pfx + ".c1"], w=[pfx + ".c1"], tag=pfx + ".c")
        P.v(lambda e: e.tensor_tensor(out=ca[:, :F], in0=ca[:, :F],
                                      in1=c1[:, :F], op=ALU.add),
            r=[pfx + ".ca", pfx + ".c1"], w=[pfx + ".ca"], tag=pfx + ".add")
        P.v(lambda e: e.tensor_scalar(out=cc[:, :F], in0=sq2[:, :F],
                                      scalar1=1.5625, scalar2=None,
                                      op0=ALU.is_gt),
            r=[pfx + ".sq", pfx + ".cc"], w=[pfx + ".cc"], tag=pfx + ".c")
        P.v(lambda e: e.tensor_scalar(out=c1[:, :F], in0=sq2[:, :F],
                                      scalar1=3.0625, scalar2=None,
                                      op0=ALU.is_gt),
            r=[pfx + ".sq", pfx + ".c1"], w=[pfx + ".c1"], tag=pfx + ".c")
        P.v(lambda e: e.tensor_tensor(out=cc[:, :F], in0=cc[:, :F],
                                      in1=c1[:, :F], op=ALU.add),
            r=[pfx + ".cc", pfx + ".c1"], w=[pfx + ".cc"], tag=pfx + ".add")
        P.v(lambda e: e.tensor_tensor(out=ca[:, :F], in0=ca[:, :F],
                                      in1=cc[:, :F], op=ALU.add),
            r=[pfx + ".ca", pfx + ".cc"], w=[pfx + ".ca"], tag=pfx + ".add")
        P.v(lambda e: e.tensor_scalar(out=cb[:, :F], in0=sq2[:, :F],
                                      scalar1=6.25, scalar2=None,
                                      op0=ALU.is_gt),
            r=[pfx + ".sq", pfx + ".cb"], w=[pfx + ".cb"], tag=pfx + ".c")
        P.v(lambda e: e.tensor_scalar(out=c1[:, :F], in0=sq2[:, :F],
                                      scalar1=12.25, scalar2=None,
                                      op0=ALU.is_gt),
            r=[pfx + ".sq", pfx + ".c1"], w=[pfx + ".c1"], tag=pfx + ".c")
        P.v(lambda e: e.tensor_tensor(out=cb[:, :F], in0=cb[:, :F],
                                      in1=c1[:, :F], op=ALU.add),
            r=[pfx + ".cb", pfx + ".c1"], w=[pfx + ".cb"], tag=pfx + ".add")
        P.v(lambda e: e.tensor_scalar(out=c1[:, :F], in0=sq2[:, :F],
                                      scalar1=25.0, scalar2=2.0,
                                      op0=ALU.is_gt, op1=ALU.mult),
            r=[pfx + ".sq", pfx + ".c1"], w=[pfx + ".c1"], tag=pfx + ".c")
        P.v(lambda e: e.tensor_tensor(out=cb[:, :F], in0=cb[:, :F],
                                      in1=c1[:, :F], op=ALU.add),
            r=[pfx + ".cb", pfx + ".c1"], w=[pfx + ".cb"], tag=pfx + ".add")
        P.v(lambda e: e.tensor_scalar(out=cb[:, :F], in0=cb[:, :F],
                                      scalar1=2.0, scalar2=None,
                                      op0=ALU.mult),
            r=[pfx + ".cb"], w=[pfx + ".cb"], tag=pfx + ".sc")
        P.v(lambda e: e.tensor_tensor(out=ca[:, :F], in0=ca[:, :F],
                                      in1=cb[:, :F], op=ALU.add),
            r=[pfx + ".ca", pfx + ".cb"], w=[pfx + ".ca"], tag=pfx + ".add")
        spb = nsp[:, :G].unsqueeze(2).broadcast_to([128, G, 16])
        P.v(lambda e: e.tensor_tensor(
                out=c1[:, :F].rearrange("p (g k) -> p g k", k=16),
                in0=ca[:, :F].rearrange("p (g k) -> p g k", k=16),
                in1=spb, op=ALU.mult),
            r=[pfx + ".ca", pfx + ".nsp", pfx + ".c1"], w=[pfx + ".c1"], tag=pfx + ".y0")
        P.g(lambda e: e.tensor_tensor(out=out_ap, in0=c1[:, :F],
                                      in1=sgn[:, :F], op=ALU.mult),
            r=[pfx + ".c1", pfx + ".sgn", out_key], w=[out_key], tag=pfx + ".y")


def build_program(probe=False, piece_table=None):
    nc = bass.Bass("TRN2", target_bir_lowering=False)
    mk = MK(nc)
    P = mk.P

    d_x = nc.dram_tensor("x", [BPC * S, E], F32, kind="ExternalInput").ap()
    d_y = nc.dram_tensor("y", [BPC * S, E], F32, kind="ExternalOutput").ap()
    dw = {}
    for nm, shp in [("wq", [E, DC]), ("bq", [DC]), ("wqc", [DC, E - DR]),
                    ("bqc", [E - DR]), ("wqr", [E, DR]), ("bqr", [DR]),
                    ("wkv", [E, DC]), ("bkv", [DC]), ("wkr", [E, DR]),
                    ("bkr", [DR]), ("wkc", [DC, E - DR]), ("bkc", [E - DR]),
                    ("wv", [DC, E]), ("bv", [E]), ("wo", [E, E]), ("bo", [E]),
                    ("wg", [I, E]), ("wu", [I, E]), ("wd", [E, I])]:
        dw[nm] = nc.dram_tensor(nm, shp, F32, kind="ExternalInput").ap()
    d_cos = nc.dram_tensor("costab", [128, S], F32, kind="ExternalInput").ap()
    d_sin = nc.dram_tensor("sintab", [128, S], F32, kind="ExternalInput").ap()
    d_ide = nc.dram_tensor("identity", [128, 128], F32, kind="ExternalInput").ap()
    d_ltm = nc.dram_tensor("ltmask", [128, 128], F32, kind="ExternalInput").ap()

    _dbg = os.environ.get("MK_DEBUG", "") == "1"
    _kind = "ExternalOutput" if _dbg else "Internal"
    s_xqT = nc.dram_tensor("s_xqT", [BPC, 128, NT * E], BF16, kind=_kind).ap()
    s_qp = nc.dram_tensor("s_qp", [BPC, 8, 128, S], BF16, kind=_kind).ap()
    s_kp = nc.dram_tensor("s_kp", [BPC, 8, 128, S], BF16, kind=_kind).ap()
    s_v = nc.dram_tensor("s_v", [BPC, NT, 128, E], BF16, kind=_kind).ap()
    s_at = nc.dram_tensor("s_at", [BPC, H, HD, S], BF16, kind=_kind).ap()
    s_wg = nc.dram_tensor("s_wg", [E, I], BF16, kind=_kind).ap()
    s_wu = nc.dram_tensor("s_wu", [E, I], BF16, kind=_kind).ap()
    s_wd = nc.dram_tensor("s_wd", [I, E], BF16, kind=_kind).ap()
    s_ng = nc.dram_tensor("s_ng", [BPC, H], F32, kind=_kind).ap()
    d_dbg = (nc.dram_tensor("dbg", [8, 128, 1024], F32, kind="ExternalOutput").ap()
             if _dbg else None)

    def dbg_snap(k, ap_, rkeys, col0=0):
        if d_dbg is not None:
            P.dma("cout", lambda e, k=k, ap_=ap_, col0=col0: e.dma_start(
                    out=d_dbg[k][0:ap_.shape[0], col0:col0 + ap_.shape[-1]],
                    in_=ap_),
                  r=list(rkeys), w=[f"dbg{k}"], tag=f"dbg.{k}")
    mk.dbg_snap = dbg_snap

    ident_f = mk.sb("ident_f", [128, 128], F32)
    ident_b = mk.sb("ident_b", [128, 128], BF16)
    ltm_b = mk.sb("ltm_b", [128, 128], BF16)
    ones_bf = mk.sb("ones_bf", [1, 128], BF16)
    bo_row = mk.sb("bo_row", [1, E], BF16)
    cst = mk.sb("cst", [128, 1], F32)
    nc.const_aps.register(F32, 1e-5, cst[:, 0:1])
    wo_s = mk.sb("wo_s", [128, EC * E], BF16)

    def alloc_qset(pfx, Fw):
        d = {"pfx": pfx}
        names = []
        gw = Fw // 16
        for nm, w, dt in (("mx", gw, F32), ("s", gw, F32), ("r", gw, F32),
                          ("nsp", gw, F32), ("t", Fw, F32), ("sq", Fw, F32),
                          ("sgn", Fw, BF16), ("ca", Fw, BF16),
                          ("cb", Fw, BF16), ("cc", Fw, BF16),
                          ("c1", Fw, BF16)):
            d[nm] = mk.sb(f"{pfx}_{nm}", [128, w], dt)
            names.append(f"{pfx}_{nm}")
        return d, names

    pA = mk.ps("pA", [128, 1024], F32)
    pB = mk.ps("pB", [128, 1024], F32)
    pAV = mk.ps("pAV", [65, S], F32)
    KA0, KA1, KB0, KB1 = "pA0", "pA1", "pB0", "pB1"

    cosT = mk.sb("cosT", [128, S], F32)
    sinT = mk.sb("sinT", [128, S], F32)
    stgw = mk.sb("stgw", [128, E], F32)
    w_sb = {"wo": wo_s}
    wspec = {"wq": (EC, DC), "wqc": (2, E - DR), "wqr": (EC, DR),
             "wkv": (EC, DC), "wkr": (EC, DR), "wkc": (2, E - DR),
             "wv": (2, E)}
    for nm, (kchunks, odim) in wspec.items():
        w_sb[nm] = mk.sb(nm + "_s", [128, kchunks * odim], BF16)
    b_sb = {}
    for nm, n in [("bq", DC), ("bqc", E - DR), ("bqr", DR), ("bkv", DC),
                  ("bkr", DR), ("bkc", E - DR), ("bv", E)]:
        b_sb[nm] = mk.sb(nm + "_s", [128, n // 128], F32)

    P.dma("cw", lambda e: e.dma_start(out=ident_f[:], in_=d_ide), w=["ident_f"])
    P.v(lambda e: e.tensor_copy(out=ident_b[:], in_=ident_f[:]),
        r=["ident_f", "ident_b"], w=["ident_b"])
    P.dma("cw", lambda e: e.dma_start(out=cosT[:], in_=d_cos), w=["cosT"])
    P.dma("cw", lambda e: e.dma_start(out=sinT[:], in_=d_sin), w=["sinT"])
    P.v(lambda e: e.memset(cst[:], 1e-5), w=["cst"])
    P.v(lambda e: e.memset(ones_bf[:], 1.0), w=["ones_bf"])
    wspec2 = dict(wspec)
    wspec2["wo"] = (EC, E)
    if os.environ.get("MK_NOWL", "0") == "1":
        wspec2 = {}
    for nm, (kchunks, odim) in wspec2.items():
        for k in range(kchunks):
            P.dma("cw", lambda e, nm=nm, k=k, odim=odim: e.dma_start(
                    out=stgw[:, 0:odim], in_=dw[nm][128 * k:128 * (k + 1), :]),
                  r=["stgw"], w=["stgw"], tag=f"ldw.{nm}")
            P.v(lambda e, nm=nm, k=k, odim=odim: e.tensor_copy(
                    out=w_sb[nm][:, k * odim:(k + 1) * odim],
                    in_=stgw[:, 0:odim]),
                r=["stgw"], w=[nm + "_s"], tag=f"cvw.{nm}")
    for nm, n in ([] if os.environ.get("MK_NOWL", "0") == "1" else
                  [("bq", DC), ("bqc", E - DR), ("bqr", DR), ("bkv", DC),
                   ("bkr", DR), ("bkc", E - DR), ("bv", E)]):
        P.dma("cw", lambda e, nm=nm, n=n: e.dma_start(
                out=b_sb[nm][:, 0:n // 128],
                in_=dw[nm].rearrange("(k p) -> p k", p=128),
                allow_slow_non_contiguous=True),
              w=[nm + "_s"], tag=f"ldb.{nm}")
    P.dma("cw", lambda e: e.dma_start(out=stgw[0:1, 0:E],
                                       in_=dw["bo"].rearrange("e -> () e")),
          r=["stgw"], w=["stgw"], tag="ldb.bo")
    P.v(lambda e: e.tensor_copy(out=bo_row[:], in_=stgw[0:1, 0:E]),
        r=["stgw"], w=["bo_row"])
    P.dma("cw", lambda e: e.dma_start(out=stgw[:, 0:128], in_=d_ltm),
          r=["stgw"], w=["stgw"])
    P.v(lambda e: e.tensor_copy(out=ltm_b[:], in_=stgw[:, 0:128]),
        r=["stgw"], w=["ltm_b"])

    P.phase = "wquant"

    qw_sets = [alloc_qset("qw0", 1024), alloc_qset("qw1", 1024)]
    wstg2 = [mk.sb("wstg0", [128, 1024], F32), mk.sb("wstg1", [128, 1024], F32)]
    wqf2 = [mk.sb("wqf0", [128, 1024], F32), mk.sb("wqf1", [128, 1024], F32)]
    wqT2 = [mk.sb("wqT0", [128, 1024], BF16), mk.sb("wqT1", [128, 1024], BF16)]
    _pc = [0]

    def quant_weight(dsrc, dscr, rows, cols, nm):
        RC = rows // 128
        pieces = []
        c0 = 0
        while c0 < cols:
            cw = min(1024, cols - c0)
            pieces.append((c0, cw))
            c0 += cw
        for kc in range(RC):
            for c0, cw in pieces:
                par = _pc[0] % 2
                _pc[0] += 1
                wstg, wqf, wqT = wstg2[par], wqf2[par], wqT2[par]
                kst, kqf, kqT = f"wstg{par}", f"wqf{par}", f"wqT{par}"
                slots = ((pA[:, 0:128], KA0), (pA[:, 512:640], KA1)) \
                    if par == 0 else ((pB[:, 0:128], KB0), (pB[:, 512:640], KB1))
                P.dma("cw", lambda e, kc=kc, c0=c0, cw=cw, dsrc=dsrc,
                      wstg=wstg: e.dma_start(
                          out=wstg[:, 0:cw],
                          in_=dsrc[128 * kc:128 * (kc + 1), c0:c0 + cw]),
                      r=[kst], w=[kst], tag=f"qw.ld.{nm}")
                mk.emit_quant(wstg[:, 0:cw], wqf[:, 0:cw], cw, [kst], kqf,
                              qd=qw_sets[par][0])
                for cc in range(cw // 128):
                    pslot, kps = slots[cc % 2]
                    P.t(lambda e, cc=cc, pslot=pslot, wqf=wqf: e.transpose(
                            pslot, wqf[:, cc * 128:(cc + 1) * 128], ident_f[:]),
                        r=[kqf, "ident_f", kps], w=[kps], tag=f"qw.tr.{nm}")
                    P.v(lambda e, cc=cc, pslot=pslot, wqT=wqT: e.tensor_copy(
                            out=wqT[:, cc * 128:(cc + 1) * 128], in_=pslot),
                        r=[kps, kqT], w=[kqT, kps], tag=f"qw.ev.{nm}")
                P.dma("csp", lambda e, kc=kc, c0=c0, cw=cw, dscr=dscr,
                      wqT=wqT: e.dma_start(
                          out=dscr[c0:c0 + cw, 128 * kc:128 * (kc + 1)]
                          .rearrange("(c p) r -> p c r", p=128),
                          in_=wqT[:, 0:cw].rearrange("p (c r) -> p c r", r=128)),
                      r=[kqT], w=[f"scr_{nm}"], tag=f"qw.st.{nm}")

    if os.environ.get("MK_NOW", "0") != "1":
        quant_weight(dw["wg"], s_wg, I, E, "wg")
        quant_weight(dw["wu"], s_wu, I, E, "wu")
        quant_weight(dw["wd"], s_wd, E, I, "wd")
    mk.free(["wstg0", "wstg1", "wqf0", "wqf1", "wqT0", "wqT1"]
            + qw_sets[0][1] + qw_sets[1][1])

    NT1 = int(os.environ.get("MK_P1TILES", str(NT)))

    def phase_p1(b, T):
        xnT, xt, xnf, xqf, lnst, xqTall = (T["xnT"], T["xt"], T["xnf"],
                                           T["xqf"], T["ln_sc"], T["xqTall"])
        for i in range(NT1):
            xb = xt[i % 2]
            kxt = f"xt{i % 2}"
            P.dma("cin", lambda e, i=i, xb=xb, b=b: e.dma_start(
                    out=xb[:], in_=d_x[b * S + 128 * i:b * S + 128 * (i + 1), :]),
                  r=[kxt], w=[kxt], tag=f"p1.ldx.{i}")
            P.v(lambda e, xb=xb, i=i: e.reduce_sum(out=lnst[:, i:i + 1],
                                                   in_=xb[:], axis=AX.X),
                r=[kxt, "lnst"], w=["lnst"], tag="p1.sum")
            P.s(lambda e, xb=xb, i=i: e.activation(
                    out=pB[:, 0:E], in_=xb[:], func=AF.Square,
                    accum_out=lnst[:, 16 + i:17 + i]),
                r=[kxt, KB0, KB1, "lnst"], w=[KB0, KB1, "lnst"], tag="p1.ssq")
        P.v(lambda e: e.tensor_scalar_mul(out=lnst[:, 32:48],
                                          in0=lnst[:, 0:16],
                                          scalar1=-1.0 / E),
            r=["lnst"], w=["lnst"], tag="p1.negm")
        P.s(lambda e: e.activation(out=lnst[:, 48:64], in_=lnst[:, 32:48],
                                   func=AF.Square),
            r=["lnst"], w=["lnst"], tag="p1.m2")
        P.v(lambda e: e.tensor_scalar_mul(out=lnst[:, 0:16],
                                          in0=lnst[:, 16:32],
                                          scalar1=1.0 / E),
            r=["lnst"], w=["lnst"], tag="p1.ex2")
        if b == 0:
            mk.dbg_snap(5, lnst[:, 0:64], ["lnst"])
        P.v(lambda e: e.tensor_tensor(out=lnst[:, 0:16], in0=lnst[:, 0:16],
                                      in1=lnst[:, 48:64], op=ALU.subtract),
            r=["lnst"], w=["lnst"], tag="p1.var")
        if b == 0:
            mk.dbg_snap(6, lnst[:, 0:64], ["lnst"])
        P.s(lambda e: e.activation(out=lnst[:, 48:64], in_=lnst[:, 0:16],
                                   func=AF.Ln, scale=1.0, bias=1e-5),
            r=["lnst", "cst"], w=["lnst"], tag="p1.lnv")
        P.s(lambda e: e.activation(out=lnst[:, 48:64], in_=lnst[:, 48:64],
                                   func=AF.Exp, scale=-0.5),
            r=["lnst"], w=["lnst"], tag="p1.rstd")
        for i in range(NT1):
            xb = xt[i % 2]
            kxt = f"xt{i % 2}"
            P.dma("cin", lambda e, i=i, xb=xb, b=b: e.dma_start(
                    out=xb[:], in_=d_x[b * S + 128 * i:b * S + 128 * (i + 1), :]),
                  r=[kxt], w=[kxt], tag=f"p1.ldx2.{i}")
            P.v(lambda e, xb=xb, i=i: e.tensor_tensor(
                    out=xnf[:], in0=xb[:],
                    in1=lnst[:, 32 + i:33 + i].broadcast_to([128, E]),
                    op=ALU.add),
                r=[kxt, "lnst", "xnf"], w=["xnf"], tag="p1.xnf")
            P.v(lambda e, i=i: e.tensor_tensor(
                    out=xnf[:], in0=xnf[:],
                    in1=lnst[:, 48 + i:49 + i].broadcast_to([128, E]),
                    op=ALU.mult),
                r=["lnst", "xnf"], w=["xnf"], tag="p1.xnf2")
            for c in range(EC):
                kps = KA0 if c % 2 == 0 else KA1
                pslot = pA[:, 0:128] if c % 2 == 0 else pA[:, 512:640]
                P.t(lambda e, c=c, pslot=pslot: e.transpose(
                        pslot, xnf[:, c * 128:(c + 1) * 128], ident_f[:]),
                    r=["xnf", "ident_f", kps], w=[kps], tag=f"p1.trx.{c}")
                P.v(lambda e, c=c, i=i, pslot=pslot: e.tensor_copy(
                        out=xnT[c][:, 128 * i:128 * (i + 1)], in_=pslot),
                    r=[kps, f"xnT{c}"], w=[f"xnT{c}", kps], tag=f"p1.evx.{c}")
            if b == 0 and i == 0:
                mk.dbg_snap(0, xnf[:], ["xnf"])
            mk.emit_quant(xnf[:], xqf[:], E, ["xnf"], "xqf")
            if b == 0 and i == 0:
                mk.dbg_snap(1, mk.q["t"][:, 0:E], ["q.t"])
                mk.dbg_snap(2, mk.q["sq"][:, 0:E], ["q.sq"])
                mk.dbg_snap(3, xqf[:], ["xqf"])
                mk.dbg_snap(4, lnst[:, 0:64], ["lnst"])
            for c in range(EC):
                kps = KA0 if c % 2 == 0 else KA1
                pslot = pA[:, 0:128] if c % 2 == 0 else pA[:, 512:640]
                P.t(lambda e, c=c, pslot=pslot: e.transpose(
                        pslot, xqf[:, c * 128:(c + 1) * 128], ident_f[:]),
                    r=["xqf", "ident_f", kps], w=[kps], tag=f"p1.trq.{c}")
                P.v(lambda e, c=c, i=i, pslot=pslot: e.tensor_copy(
                        out=xqTall[:, i * E + c * 128:i * E + (c + 1) * 128],
                        in_=pslot),
                    r=[kps, "xqTall"], w=["xqTall", kps], tag=f"p1.evq.{c}")
        P.dma("csp", lambda e, b=b, xqTall=xqTall: e.dma_start(
                out=s_xqT[b], in_=xqTall[:]),
              r=["xqTall"], w=[f"s_xqT{b}"], tag="p1.spill")

    def phase_p2(b, T):
        xnT = T["xnT"]
        cqT, ckvT, qrp, krp, rpre = (T["cqT"], T["ckvT"], T["qrp"], T["krp"],
                                     T["rpre"])
        qpair, kpair = T["qpair"], T["kpair"]
        nstat, nrow, ngrow, vevac, vstg = (T["nstat"], T["nrow"], T["ngrow"],
                                           T["vevac"], T["vstg"])
        NC = S // 512
        specA = [("cq", "wq", "bq", 2, DC, KA0, pA[:, 0:512], cqT, "cqT",
                  AF.Gelu),
                 ("ckv", "wkv", "bkv", 2, DC, KA1, pA[:, 512:1024], ckvT,
                  "ckvT", AF.Gelu),
                 ("qr", "wqr", "bqr", 1, DR, KB0, pB[:, 0:512], qrp, "qrp",
                  AF.Identity),
                 ("kr", "wkr", "bkr", 1, DR, KB1, pB[:, 512:1024], krp, "krp",
                  AF.Identity)]
        for n in range(NC):
            for nm, wname, bname, mch, odim, kps, pslot, dstT, dkey, act in specA:
                for m in range(mch):
                    for kc in range(EC):
                        P.t(lambda e, wname=wname, m=m, kc=kc, n=n, odim=odim,
                            pslot=pslot: e.matmul(
                                pslot,
                                w_sb[wname][:, kc * odim + 128 * m:
                                            kc * odim + 128 * m + 128],
                                xnT[kc][:, 512 * n:512 * (n + 1)],
                                start=(kc == 0), stop=(kc == EC - 1)),
                            r=[wname + "_s", f"xnT{kc}", kps], w=[kps],
                            tag=f"p2a.{nm}")
                    if mch == 2:
                        P.s(lambda e, m=m, n=n, dstT=dstT, bname=bname,
                            pslot=pslot, act=act: e.activation(
                                out=dstT[:, m * S + 512 * n:
                                         m * S + 512 * (n + 1)],
                                in_=pslot, func=act,
                                bias=b_sb[bname][:, m:m + 1]),
                            r=[kps, bname + "_s", dkey], w=[dkey, kps],
                            tag=f"p2a.ev.{nm}")
                    else:
                        P.s(lambda e, n=n, dstT=dstT, bname=bname, pslot=pslot:
                            e.activation(
                                out=dstT[:, 512 * n:512 * (n + 1)],
                                in_=pslot, func=AF.Identity,
                                bias=b_sb[bname][:, 0:1]),
                            r=[kps, bname + "_s", dkey], w=[dkey, kps],
                            tag=f"p2a.ev.{nm}")
        for nm, tile, key in [("q", qrp, "qrp"), ("k", krp, "krp")]:
            P.dma("csp", lambda e, tile=tile: e.dma_start(
                    out=rpre[0:64, :], in_=tile[64:128, :]),
                  r=[key, "rpre"], w=["rpre"], tag="p2r.sw0")
            P.dma("csp", lambda e, tile=tile: e.dma_start(
                    out=rpre[64:128, :], in_=tile[0:64, :]),
                  r=[key, "rpre"], w=["rpre"], tag="p2r.sw1")
            P.v(lambda e, tile=tile: e.tensor_tensor(
                    out=tile[:, :], in0=tile[:, :], in1=cosT[:],
                    op=ALU.mult),
                r=[key, "cosT"], w=[key], tag="p2r.c")
            P.v(lambda e: e.tensor_tensor(
                    out=rpre[:, :], in0=rpre[:, :], in1=sinT[:],
                    op=ALU.mult),
                r=["rpre", "sinT"], w=["rpre"], tag="p2r.s")
            P.v(lambda e, tile=tile: e.tensor_tensor(
                    out=tile[0:64, :], in0=tile[0:64, :], in1=rpre[0:64, :],
                    op=ALU.subtract),
                r=[key, "rpre"], w=[key], tag="p2r.sub")
            P.v(lambda e, tile=tile: e.tensor_tensor(
                    out=tile[64:128, :], in0=tile[64:128, :],
                    in1=rpre[64:128, :], op=ALU.add),
                r=[key, "rpre"], w=[key], tag="p2r.add")
        for p in range(8):
            qb, kb = qpair[p % 2], kpair[p % 2]
            kqb, kkb = f"qpair{p % 2}", f"kpair{p % 2}"
            if p < 7:
                for srcT, skey, wname, bname, dst, dkey in [
                        (cqT, "cqT", "wqc", "bqc", qb, kqb),
                        (ckvT, "ckvT", "wkc", "bkc", kb, kkb)]:
                    for n in range(NC):
                        for kc in range(2):
                            P.t(lambda e, p=p, kc=kc, n=n, srcT=srcT,
                                wname=wname: e.matmul(
                                    pB[:, 0:512],
                                    w_sb[wname][:, kc * (E - DR) + 128 * p:
                                                kc * (E - DR) + 128 * p + 128],
                                    srcT[:, kc * S + 512 * n:
                                         kc * S + 512 * (n + 1)],
                                    start=(kc == 0), stop=(kc == 1)),
                                r=[wname + "_s", skey, KB0], w=[KB0],
                                tag=f"p2h.{p}")
                        P.s(lambda e, p=p, n=n, dst=dst, bname=bname:
                            e.activation(
                                out=dst[:, 512 * n:512 * (n + 1)],
                                in_=pB[:, 0:512], func=AF.Gelu,
                                bias=b_sb[bname][:, p:p + 1]),
                            r=[KB0, bname + "_s", dkey], w=[dkey, KB0],
                            tag=f"p2h.ev.{p}")
            else:
                for tile, key, dst, dkey in [(qrp, "qrp", qb, kqb),
                                             (krp, "krp", kb, kkb)]:
                    P.s(lambda e, tile=tile, dst=dst: e.activation(
                            out=dst[:], in_=tile[:], func=AF.Gelu),
                        r=[key, dkey], w=[dkey], tag="p2h.rope")
            for side, buf, bkey in [(0, qb, kqb), (1, kb, kkb)]:
                P.v(lambda e, buf=buf: e.reduce_max(
                        out=nstat[:, 0:1], in_=buf[:], axis=AX.X,
                        apply_absolute_value=True),
                    r=[bkey, "nstat"], w=["nstat"], tag="p2n.mx")
                P.s(lambda e: e.activation(out=nstat[:, 1:2],
                                           in_=nstat[:, 0:1], func=AF.Square),
                    r=["nstat"], w=["nstat"], tag="p2n.sq")
                P.t(lambda e: e.transpose(pB[0:1, 0:128], nstat[:, 1:2],
                                          ident_f[:]),
                    r=["nstat", "ident_f", KB0], w=[KB0], tag="p2n.tr")
                P.v(lambda e, side=side: e.reduce_sum(
                        out=nrow[0:1, side * 2:side * 2 + 1],
                        in_=pB[0:1, 0:64], axis=AX.X),
                    r=[KB0, "nrow"], w=["nrow"], tag="p2n.s0")
                P.v(lambda e, side=side: e.reduce_sum(
                        out=nrow[0:1, side * 2 + 1:side * 2 + 2],
                        in_=pB[0:1, 64:128], axis=AX.X),
                    r=[KB0, "nrow"], w=["nrow", KB0], tag="p2n.s1")
            P.v(lambda e: e.tensor_tensor(out=nrow[0:1, 4:6],
                                          in0=nrow[0:1, 0:2],
                                          in1=nrow[0:1, 2:4], op=ALU.add),
                r=["nrow"], w=["nrow"], tag="p2n.qk")
            P.v(lambda e, p=p: e.tensor_scalar_mul(
                    out=ngrow[0:1, 2 * p:2 * p + 2], in0=nrow[0:1, 4:6],
                    scalar1=-0.0625),
                r=["nrow", "ngrow"], w=["ngrow"], tag="p2n.c")
            if b == 0 and p == 0:
                mk.dbg_snap(7, nstat[:, 0:2], ["nstat"])
                mk.dbg_snap(7, nrow[0:1, 0:8], ["nrow"], col0=8)
            P.dma("csp", lambda e, p=p, qb=qb, b=b: e.dma_start(
                    out=s_qp[b, p], in_=qb[:]),
                  r=[kqb], w=[f"s_qp{b}.{p}"], tag="p2.spq")
            P.dma("csp", lambda e, p=p, kb=kb, b=b: e.dma_start(
                    out=s_kp[b, p], in_=kb[:]),
                  r=[kkb], w=[f"s_kp{b}.{p}"], tag="p2.spk")
        P.dma("csp", lambda e, b=b, ngrow=ngrow: e.dma_start(
                out=s_ng[b].rearrange("h -> () h"), in_=ngrow[0:1, :]),
              r=["ngrow"], w=[f"s_ng{b}"], tag="p2.spng")
        for n in range(NC):
            for m in range(EC):
                for kc in range(2):
                    P.t(lambda e, m=m, kc=kc, n=n: e.matmul(
                            pA[:, 0:512],
                            w_sb["wv"][:, kc * E + 128 * m:
                                       kc * E + 128 * m + 128],
                            ckvT[:, kc * S + 512 * n:kc * S + 512 * (n + 1)],
                            start=(kc == 0), stop=(kc == 1)),
                        r=["wv_s", "ckvT", KA0], w=[KA0], tag="p2v.mm")
                P.s(lambda e, m=m: e.activation(
                        out=vevac[:, 0:512], in_=pA[:, 0:512],
                        func=AF.Identity, bias=b_sb["bv"][:, m:m + 1]),
                    r=[KA0, "bv_s", "vevac"], w=["vevac", KA0], tag="p2v.ev")
                for j in range(4):
                    kps = KA1 if j % 2 == 0 else KB1
                    pslot = pA[:, 512:640] if j % 2 == 0 else pB[:, 512:640]
                    P.t(lambda e, j=j, pslot=pslot: e.transpose(
                            pslot, vevac[:, j * 128:(j + 1) * 128],
                            ident_f[:]),
                        r=["vevac", "ident_f", kps], w=[kps], tag="p2v.tr")
                    P.v(lambda e, m=m, j=j, pslot=pslot: e.tensor_copy(
                            out=vstg[j][:, m * 128:(m + 1) * 128], in_=pslot),
                        r=[kps, f"vstg{j}"], w=[f"vstg{j}", kps],
                        tag="p2v.evt")
            for j in range(4):
                kt = 4 * n + j
                P.dma("csp", lambda e, kt=kt, j=j, b=b, vstg=vstg:
                      e.dma_start(out=s_v[b, kt], in_=vstg[j][:]),
                      r=[f"vstg{j}"], w=[f"s_v{b}.{kt}"], tag="p2v.sp")

    def phase_p3(b, T):
        vres, vaug, ptil, avs, otm, recl, athd, negc = (
            T["vres"], T["vaug"], T["ptil"], T["avs"], T["otm"], T["recl"],
            T["athd"], T["negc"])
        qpair, kpair = T["qpair3"], T["kpair3"]
        P.dma("cld", lambda e, b=b, negc=negc: e.dma_start(
                out=negc[:],
                in_=s_ng[b].rearrange("h -> () h").broadcast_to([128, H])),
              r=[f"s_ng{b}", "negc"], w=["negc"], tag="p3.ldng")
        for kt in range(NT):
            P.dma("cld", lambda e, kt=kt, b=b, vres=vres: e.dma_start(
                    out=vres[kt][:], in_=s_v[b, kt]),
                  r=[f"s_v{b}.{kt}", f"vres{kt}"], w=[f"vres{kt}"],
                  tag=f"p3.ldv.{kt}")
        for kt in range(NT):
            P.v(lambda e, kt=kt, vaug=vaug: e.memset(
                    vaug[:, kt * 65 + 64:kt * 65 + 65], 1.0),
                r=["vaug"], w=["vaug"], tag="p3.vones")
        for p in range(8):
            qb, kb = qpair[p % 2], kpair[p % 2]
            kqb, kkb = f"q3p{p % 2}", f"k3p{p % 2}"
            P.dma("cld", lambda e, p=p, qb=qb, b=b: e.dma_start(
                    out=qb[:], in_=s_qp[b, p]),
                  r=[f"s_qp{b}.{p}", kqb], w=[kqb], tag=f"p3.ldq.{p}")
            P.dma("cld", lambda e, p=p, kb=kb, b=b: e.dma_start(
                    out=kb[:], in_=s_kp[b, p]),
                  r=[f"s_kp{b}.{p}", kkb], w=[kkb], tag=f"p3.ldk.{p}")
            for half in range(2):
                h = 2 * p + half
                base = 64 * half
                for kt in range(NT):
                    P.g(lambda e, kt=kt, h=h, vres=vres, vaug=vaug:
                        e.tensor_copy(
                            out=vaug[:, kt * 65:kt * 65 + 64],
                            in_=vres[kt][:, h * HD:(h + 1) * HD]),
                        r=[f"vres{kt}", "vaug"], w=["vaug"],
                        tag=f"p3.va.{kt}")
                SLOTS = [(pA, 0, KA0), (pA, 512, KA1),
                         (pB, 0, KB0), (pB, 512, KB1)]
                chunks = {}
                for kt in range(NT):
                    lst = []
                    c0 = 128 * kt
                    while c0 < S:
                        c1 = min((c0 // 512 + 1) * 512, S)
                        lst.append((c0, c1))
                        c0 = c1
                    chunks[kt] = lst
                scnt = [0]

                def emit_qk(kt, h=h, base=base, qb=qb, kb=kb, kqb=kqb,
                            kkb=kkb, negc=negc):
                    pt_t = ptil[kt % 2]
                    kpt = f"ptil{kt % 2}"
                    for (c0, c1) in chunks[kt]:
                        pp, off, kps = SLOTS[scnt[0] % 4]
                        scnt[0] += 1
                        P.t(lambda e, kt=kt, c0=c0, c1=c1, pp=pp, off=off,
                            base=base, qb=qb, kb=kb: e.matmul(
                                pp[:, off:off + (c1 - c0)],
                                kb[base:base + 64, 128 * kt:128 * (kt + 1)],
                                qb[base:base + 64, c0:c1],
                                start=True, stop=True),
                            r=[kqb, kkb, kps], w=[kps],
                            tag=f"p3B.{h}.{kt}")
                        P.s(lambda e, c0=c0, c1=c1, pp=pp, off=off, h=h,
                            pt_t=pt_t, negc=negc: e.activation(
                                out=pt_t[:, c0:c1],
                                in_=pp[:, off:off + (c1 - c0)],
                                func=AF.Exp, scale=0.125,
                                bias=negc[:, h:h + 1]),
                            r=[kps, "negc", kpt], w=[kpt, kps],
                            tag=f"p3E.{h}.{kt}")
                    q0 = 128 * kt
                    P.v(lambda e, q0=q0, pt_t=pt_t: e.tensor_tensor(
                            out=pt_t[:, q0:q0 + 128],
                            in0=pt_t[:, q0:q0 + 128], in1=ltm_b[:],
                            op=ALU.mult),
                        r=[kpt, "ltm_b"], w=[kpt], tag=f"p3M.{h}.{kt}")

                def emit_av(kt, h=h, vaug=vaug):
                    pt_t = ptil[kt % 2]
                    kpt = f"ptil{kt % 2}"
                    for (c0, c1) in chunks[kt]:
                        P.t(lambda e, kt=kt, c0=c0, c1=c1, pt_t=pt_t,
                            vaug=vaug: e.matmul(
                                pAV[:, c0:c1],
                                vaug[:, kt * 65:(kt + 1) * 65],
                                pt_t[:, c0:c1], start=(kt == 0),
                                stop=(kt == NT - 1
                                      or c1 <= 128 * (kt + 1))),
                            r=["vaug", kpt, "pAV"], w=["pAV"],
                            tag=f"p3AV.{h}.{kt}")

                emit_qk(0)
                for kt in range(1, NT):
                    emit_qk(kt)
                    emit_av(kt - 1)
                emit_av(NT - 1)
                for c in range(0, S, 512):
                    P.s(lambda e, c=c, avs=avs: e.activation(
                            out=avs[:, c:c + 512], in_=pAV[:, c:c + 512],
                            func=AF.Copy),
                        r=["pAV", "avs"], w=["avs", "pAV"],
                        tag=f"p3.evav.{h}")
                for j in range(NT):
                    pp1, off1, k1 = SLOTS[(2 * j) % 4]
                    pp2, off2, k2 = SLOTS[(2 * j + 1) % 4]
                    jj = j % 2
                    krc, kot = f"recl{jj}", f"otm{jj}"
                    P.t(lambda e, j=j, avs=avs, pp1=pp1, off1=off1:
                        e.transpose(
                            pp1[:, off1:off1 + 65],
                            avs[:, 128 * j:128 * (j + 1)],
                            ident_f[0:65, 0:65]),
                        r=["avs", "ident_f", k1], w=[k1],
                        tag=f"p3.tr1.{h}")
                    P.v(lambda e, recl=recl, jj=jj, pp1=pp1, off1=off1:
                        e.reciprocal(
                            out=recl[:, jj:jj + 1],
                            in_=pp1[:, off1 + 64:off1 + 65]),
                        r=[k1, krc], w=[krc], tag=f"p3.rc.{h}")
                    P.v(lambda e, otm=otm, recl=recl, jj=jj, pp1=pp1,
                        off1=off1: e.tensor_tensor(
                            out=otm[:, 64 * jj:64 * jj + 64],
                            in0=pp1[:, off1:off1 + 64],
                            in1=recl[:, jj:jj + 1].broadcast_to([128, 64]),
                            op=ALU.mult),
                        r=[k1, krc, kot], w=[kot, k1],
                        tag=f"p3.dv.{h}")
                    P.t(lambda e, otm=otm, jj=jj, pp2=pp2, off2=off2:
                        e.transpose(
                            pp2[0:64, off2:off2 + 128],
                            otm[:, 64 * jj:64 * jj + 64], ident_f[:]),
                        r=[kot, "ident_f", k2], w=[k2],
                        tag=f"p3.tr2.{h}")
                    P.v(lambda e, j=j, athd=athd, pp2=pp2, off2=off2:
                        e.tensor_copy(
                            out=athd[:, 128 * j:128 * (j + 1)],
                            in_=pp2[0:64, off2:off2 + 128]),
                        r=[k2, "athd"], w=["athd", k2], tag=f"p3.ev2.{h}")
                P.dma("csp", lambda e, h=h, b=b, athd=athd: e.dma_start(
                        out=s_at[b, h], in_=athd[:]),
                      r=["athd"], w=[f"s_at{b}.{h}"], tag=f"p3.spat.{h}")

    PH = os.environ.get("MK_PHASES", "1234")
    NSEQ = int(os.environ.get("MK_NSEQ", str(BPC)))
    for b in range(NSEQ):
        T = {}
        T["xnT"] = [mk.sb(f"xnT{c}_{b}", [128, S], BF16) for c in range(EC)]
        p1names = [f"xt0_{b}", f"xt1_{b}", f"xnf_{b}", f"xqf_{b}",
                   f"lnsc_{b}", f"xqTall_{b}"]
        T["xt"] = [mk.sb(p1names[0], [128, E], F32),
                   mk.sb(p1names[1], [128, E], F32)]
        T["xnf"] = mk.sb(p1names[2], [128, E], F32)
        T["xqf"] = mk.sb(p1names[3], [128, E], F32)
        T["ln_sc"] = mk.sb(p1names[4], [128, 64], F32)
        T["xqTall"] = mk.sb(p1names[5], [128, NT * E], BF16)
        q1set, q1names = alloc_qset(f"q1{b}", 1024)
        mk.q = q1set
        P.phase = f"p1.{b}"
        phase_p1(b, T)
        mk.free(p1names + q1names)
        if "2" not in PH:
            mk.free([f"xnT{c}_{b}" for c in range(EC)])
            continue
        p2names = [f"cqT_{b}", f"ckvT_{b}", f"qrp_{b}", f"krp_{b}",
                   f"rpre_{b}", f"qpair0_{b}", f"qpair1_{b}", f"kpair0_{b}",
                   f"kpair1_{b}", f"nstat_{b}", f"nrow_{b}", f"ngrow_{b}",
                   f"vevac_{b}", f"vstg0_{b}", f"vstg1_{b}", f"vstg2_{b}",
                   f"vstg3_{b}"]
        T["cqT"] = mk.sb(p2names[0], [128, 2 * S], BF16)
        T["ckvT"] = mk.sb(p2names[1], [128, 2 * S], BF16)
        T["qrp"] = mk.sb(p2names[2], [128, S], F32)
        T["krp"] = mk.sb(p2names[3], [128, S], F32)
        T["rpre"] = mk.sb(p2names[4], [128, S], F32)
        T["qpair"] = [mk.sb(p2names[5], [128, S], BF16),
                      mk.sb(p2names[6], [128, S], BF16)]
        T["kpair"] = [mk.sb(p2names[7], [128, S], BF16),
                      mk.sb(p2names[8], [128, S], BF16)]
        T["nstat"] = mk.sb(p2names[9], [128, 2], F32)
        T["nrow"] = mk.sb(p2names[10], [1, 128], F32)
        T["ngrow"] = mk.sb(p2names[11], [1, H], F32)
        T["vevac"] = mk.sb(p2names[12], [128, 512], F32)
        T["vstg"] = [mk.sb(p2names[13 + j], [128, E], BF16) for j in range(4)]
        P.phase = f"p2.{b}"
        phase_p2(b, T)
        mk.free(p2names + [f"xnT{c}_{b}" for c in range(EC)])
        if "3" not in PH:
            continue
        p3names = ([f"vres{kt}_{b}" for kt in range(NT)] +
                   [f"vaug_{b}", f"ptil0_{b}", f"ptil1_{b}", f"avs_{b}",
                    f"otm_{b}", f"recl_{b}", f"athd_{b}", f"negc_{b}",
                    f"q3p0_{b}", f"q3p1_{b}", f"k3p0_{b}", f"k3p1_{b}"])
        T["vres"] = [mk.sb(f"vres{kt}_{b}", [128, E], BF16)
                     for kt in range(NT)]
        T["vaug"] = mk.sb(f"vaug_{b}", [128, NT * 65], BF16)
        T["ptil"] = [mk.sb(f"ptil0_{b}", [128, S], BF16),
                     mk.sb(f"ptil1_{b}", [128, S], BF16)]
        T["avs"] = mk.sb(f"avs_{b}", [65, S], F32)
        T["otm"] = mk.sb(f"otm_{b}", [128, 128], F32)
        T["recl"] = mk.sb(f"recl_{b}", [128, 2], F32)
        T["athd"] = mk.sb(f"athd_{b}", [HD, S], BF16)
        T["negc"] = mk.sb(f"negc_{b}", [128, H], F32)
        T["qpair3"] = [mk.sb(f"q3p0_{b}", [128, S], BF16),
                       mk.sb(f"q3p1_{b}", [128, S], BF16)]
        T["kpair3"] = [mk.sb(f"k3p0_{b}", [128, S], BF16),
                       mk.sb(f"k3p1_{b}", [128, S], BF16)]
        P.phase = f"p3.{b}"
        phase_p3(b, T)
        mk.free(p3names)

    mk.free(["cosT", "sinT", "stgw"] + [nm + "_s" for nm in wspec] +
            [nm + "_s" for nm in ("bq", "bqc", "bqr", "bkv", "bkr", "bkc",
                                  "bv")] + ["pA", "pB", "pAV"])

    GT = 1
    xqg = [mk.sb(f"xqg{i}", [128, GT * E], BF16) for i in range(2)]
    atg = [mk.sb(f"atg{i}", [128, H * GT * 64], BF16) for i in range(2)]
    wgq = mk.sb("wgq", [128, EC * I], BF16)
    wuq = mk.sb("wuq", [128, EC * I], BF16)
    wdq = mk.sb("wdq", [128, IC * E], BF16)
    sg_t = [mk.sb("sg_t0", [128, 512], F32), mk.sb("sg_t1", [128, 512], F32)]
    h_t = [mk.sb("h_t0", [128, 512], F32), mk.sb("h_t1", [128, 512], F32)]
    hqf = [mk.sb("hqf0", [128, 512], BF16), mk.sb("hqf1", [128, 512], BF16)]
    hqT = [mk.sb("hqTa", [128, IC * 128], BF16),
           mk.sb("hqTb", [128, IC * 128], BF16)]
    xre = [mk.sb("xre0", [128, E], F32), mk.sb("xre1", [128, E], F32)]
    q4 = [alloc_qset("q40", 512), alloc_qset("q41", 512)]
    psG = [mk.ps("psG0", [128, 512], F32), mk.ps("psG1", [128, 512], F32)]
    psU = [mk.ps("psU0", [128, 512], F32), mk.ps("psU1", [128, 512], F32)]
    psD = mk.ps("psD", [128, E], F32)
    psT4 = [mk.ps("psT4a", [128, 1024], BF16),
            mk.ps("psT4b", [128, 1024], BF16)]

    def phase_p4():
        pend = []
        for c in range(EC):
            P.dma("cw", lambda e, c=c: e.dma_start(
                    out=wgq[:, c * I:(c + 1) * I],
                    in_=s_wg[128 * c:128 * (c + 1), :]),
                  r=["scr_wg", "wgq"], w=["wgq"], tag="p4.lwg")
            P.dma("cw", lambda e, c=c: e.dma_start(
                    out=wuq[:, c * I:(c + 1) * I],
                    in_=s_wu[128 * c:128 * (c + 1), :]),
                  r=["scr_wu", "wuq"], w=["wuq"], tag="p4.lwu")
        for c in range(IC):
            P.dma("cw", lambda e, c=c: e.dma_start(
                    out=wdq[:, c * E:(c + 1) * E],
                    in_=s_wd[128 * c:128 * (c + 1), :]),
                  r=["scr_wd", "wdq"], w=["wdq"], tag="p4.lwd")
        NTOT = BPC * NT
        CH = (I + 511) // 512
        cdefs0 = []
        for c0 in range(0, I, 512):
            cw = min(512, I - c0)
            cdefs0.append((c0, cw, (c0 // 512) % 2))

        def emit_load(t_idx):
            b, grp = divmod(t_idx, NT)
            gbuf, kg = xqg[t_idx % 2], f"xqg{t_idx % 2}"
            P.dma("cld", lambda e, b=b, grp=grp, gbuf=gbuf: e.dma_start(
                    out=gbuf[:], in_=s_xqT[b][:, grp * E:(grp + 1) * E]),
                  r=[f"s_xqT{b}", kg], w=[kg], tag=f"p4.lxq.{b}.{grp}")

        def emit_load2(t_idx):
            b, grp = divmod(t_idx, NT)
            abuf, ka = atg[t_idx % 2], f"atg{t_idx % 2}"
            for hh in range(2):
                P.dma("cld", lambda e, b=b, grp=grp, abuf=abuf, hh=hh:
                      e.dma_start(
                          out=abuf[64 * hh:64 * hh + 64, :].rearrange(
                              "d (c t) -> d c t", t=128),
                          in_=s_at[b][:, :, 128 * grp:128 * (grp + 1)]
                          .rearrange("(c hh) d t -> hh d c t", hh=2)[hh]),
                      r=[f"s_at{b}.{h}" for h in range(H)] + [ka],
                      w=[ka], tag=f"p4.lat.{b}.{grp}")
            P.dma("cin", lambda e, b=b, grp=grp, t_idx=t_idx: e.dma_start(
                    out=xre[t_idx % 2][:],
                    in_=d_x[b * S + 128 * grp:b * S + 128 * (grp + 1), :]),
                  r=[f"xre{t_idx % 2}"], w=[f"xre{t_idx % 2}"],
                  tag=f"p4.lx.{b}.{grp}")

        def emit_gu(t_idx, k):
            c0, cw, ci = cdefs0[k]
            gbuf, kg = xqg[t_idx % 2], f"xqg{t_idx % 2}"
            kG, kU = f"psG{ci}", f"psU{ci}"
            pG, pU = psG[ci], psU[ci]
            for kc in range(EC):
                P.t(lambda e, kc=kc, c0=c0, cw=cw, gbuf=gbuf, pG=pG:
                    e.matmul(
                        pG[:, 0:cw],
                        gbuf[:, kc * 128:(kc + 1) * 128],
                        wgq[:, kc * I + c0:kc * I + c0 + cw],
                        start=(kc == 0), stop=(kc == EC - 1)),
                    r=[kg, "wgq", kG], w=[kG], tag="p4.g")
            for kc in range(EC):
                P.t(lambda e, kc=kc, c0=c0, cw=cw, gbuf=gbuf, pU=pU:
                    e.matmul(
                        pU[:, 0:cw],
                        gbuf[:, kc * 128:(kc + 1) * 128],
                        wuq[:, kc * I + c0:kc * I + c0 + cw],
                        start=(kc == 0), stop=(kc == EC - 1)),
                    r=[kg, "wuq", kU], w=[kU], tag="p4.u")
            P.s(lambda e, cw=cw, pG=pG, ci=ci: e.activation(
                    out=sg_t[ci][:, 0:cw], in_=pG[:, 0:cw], func=AF.Silu),
                r=[kG, f"sg{ci}"], w=[f"sg{ci}", kG], tag="p4.si")
            P.v(lambda e, cw=cw, pU=pU, ci=ci: e.tensor_tensor(
                    out=h_t[ci][:, 0:cw], in0=sg_t[ci][:, 0:cw],
                    in1=pU[:, 0:cw], op=ALU.mult),
                r=[f"sg{ci}", kU, f"h{ci}"], w=[f"h{ci}", kU], tag="p4.h")

        def emit_qtr(t_idx, k):
            c0, cw, ci = cdefs0[k]
            kq = f"hqT{t_idx % 2}"
            hq_b = hqT[t_idx % 2]
            if k == 0:
                emit_load2(t_idx)
            mk.emit_quant(h_t[ci][:, 0:cw], hqf[ci][:, 0:cw], cw,
                          [f"h{ci}"], f"hq{ci}", qd=q4[ci][0])
            for cc in range(cw // 128):
                c = (c0 + cc * 128) // 128
                tj = cc % 2
                kT = f"psT4{tj}"
                pT = psT4[tj]
                P.t(lambda e, cc=cc, pT=pT, ci=ci: e.transpose(
                        pT[:, 0:128],
                        hqf[ci][:, cc * 128:(cc + 1) * 128],
                        ident_b[:]),
                    r=[f"hq{ci}", "ident_b", kT], w=[kT], tag="p4.trh")
                P.s(lambda e, c=c, pT=pT, hq_b=hq_b: e.activation(
                        out=hq_b[:, c * 128:(c + 1) * 128],
                        in_=pT[:, 0:128], func=AF.Copy),
                    r=[kT, kq], w=[kq, kT], tag="p4.evh")

        def emit_tail(t_idx):
            b, grp = divmod(t_idx, NT)
            abuf, ka = atg[t_idx % 2], f"atg{t_idx % 2}"
            kq = f"hqT{t_idx % 2}"
            hq_b = hqT[t_idx % 2]
            for halfc in range(2):
                cl, cr = halfc * 512, halfc * 512 + 512
                for kc in range(IC):
                    P.t(lambda e, kc=kc, cl=cl, cr=cr, hq_b=hq_b: e.matmul(
                            psD[:, cl:cr],
                            hq_b[:, kc * 128:(kc + 1) * 128],
                            wdq[:, kc * E + cl:kc * E + cr],
                            start=(kc == 0), stop=False),
                        r=[kq, "wdq", "psD"], w=["psD"], tag="p4.d")
                for kc in range(EC):
                    P.t(lambda e, kc=kc, cl=cl, cr=cr, abuf=abuf: e.matmul(
                            psD[:, cl:cr],
                            abuf[:, kc * 128:(kc + 1) * 128],
                            w_sb["wo"][:, kc * E + cl:kc * E + cr],
                            start=False, stop=False),
                        r=[ka, "wo_s", "psD"], w=["psD"], tag="p4.wo")
                P.t(lambda e, cl=cl, cr=cr: e.matmul(
                        psD[:, cl:cr], ones_bf[:, 0:128],
                        bo_row[:, cl:cr], start=False, stop=True),
                    r=["ones_bf", "bo_row", "psD"], w=["psD"], tag="p4.bo")
            xr, kxr = xre[t_idx % 2], f"xre{t_idx % 2}"
            P.v(lambda e, xr=xr: e.tensor_tensor(
                    out=xr[:], in0=xr[:], in1=psD[:], op=ALU.add),
                r=[kxr, "psD"], w=[kxr, "psD"], tag="p4.cmb")
            P.dma("cout", lambda e, b=b, grp=grp, xr=xr: e.dma_start(
                    out=d_y[b * S + 128 * grp:b * S + 128 * (grp + 1), :],
                    in_=xr[:]),
                  r=[kxr], w=[kxr, "y_out"], tag=f"p4.st.{b}.{grp}")

        emitted = 0
        for g in range(NTOT * CH + 2):
            while emitted <= min(g // CH + 1, NTOT - 1):
                emit_load(emitted)
                emitted += 1
            if g >= 2:
                t_idx, k = divmod(g - 2, CH)
                emit_qtr(t_idx, k)
                if k == CH - 1:
                    emit_tail(t_idx)
            if g < NTOT * CH:
                t_idx, k = divmod(g, CH)
                emit_gu(t_idx, k)

    if "4" in PH:
        P.phase = "p4"
        phase_p4()
        P.y(lambda e: e.nop(), r=["y_out"], tag="final.wait")
    else:
        P.set_barrier()
    table = P.finalize_and_emit(probe=probe, piece_table=piece_table)
    mk.close_all()
    if probe:
        return nc, table
    return nc


def build_program2():
    _, table = build_program(probe=True)
    return build_program(piece_table=table)


def _host_tables():
    inv_freq = 1.0 / 10000.0 ** (np.arange(0, DR, 2, dtype=np.float64) / DR)
    t = np.arange(S, dtype=np.float64)
    freqs = t[:, None] * inv_freq[None, :]
    cos64 = np.cos(freqs).astype(np.float32).T
    sin64 = np.sin(freqs).astype(np.float32).T
    cos = np.ascontiguousarray(np.concatenate([cos64, cos64], axis=0))
    sin = np.ascontiguousarray(np.concatenate([sin64, sin64], axis=0))
    ident = np.eye(128, dtype=np.float32)
    ltm = np.ascontiguousarray(np.triu(np.ones((128, 128), dtype=np.float32)))
    return cos, sin, ident, ltm


_NC_CACHE = {}


def kernel(**inputs):
    if "prog" not in _NC_CACHE:
        _NC_CACHE["prog"] = build_program2()
    nc = _NC_CACHE["prog"]
    cos, sin, ident, ltm = _host_tables()
    x = np.ascontiguousarray(np.asarray(inputs["L0_attn_x"], dtype=np.float32))
    common = {
        "wq": inputs["L0_attn_wq"], "bq": inputs["L0_attn_bq"],
        "wqc": inputs["L0_attn_wqc"], "bqc": inputs["L0_attn_bqc"],
        "wqr": inputs["L0_attn_wqr"], "bqr": inputs["L0_attn_bqr"],
        "wkv": inputs["L0_attn_wkv"], "bkv": inputs["L0_attn_bkv"],
        "wkr": inputs["L0_attn_wkr"], "bkr": inputs["L0_attn_bkr"],
        "wkc": inputs["L0_attn_wkc"], "bkc": inputs["L0_attn_bkc"],
        "wv": inputs["L0_attn_wv"], "bv": inputs["L0_attn_bv"],
        "wo": inputs["L0_attn_wo"], "bo": inputs["L0_attn_bo"],
        "wg": inputs["L0_ffn_w_gate"], "wu": inputs["L0_ffn_w_up"],
        "wd": inputs["L0_ffn_w_down"],
        "costab": cos, "sintab": sin, "identity": ident, "ltmask": ltm,
    }
    common = {k: np.ascontiguousarray(np.asarray(v, dtype=np.float32))
              for k, v in common.items()}
    in_maps = []
    for c in range(NCORES):
        m = dict(common)
        m["x"] = np.ascontiguousarray(
            x[c * BPC:(c + 1) * BPC].reshape(BPC * S, E))
        in_maps.append(m)
    res = run_bass_kernel_spmd(nc, in_maps, core_ids=list(range(NCORES)))
    outs = [np.asarray(r["y"]).reshape(BPC, S, E) for r in res.results]
    return np.concatenate(outs, axis=0)


if __name__ == "__main__":
    import time
    t0 = time.time()
    nc = build_program()
    print("built in %.1fs, instructions: %d" % (time.time() - t0,
                                                nc.n_instructions()))
